# version 13
# baseline (speedup 1.0000x reference)
import numpy as np
import ml_dtypes
import concourse.bass as bass
import concourse.mybir as mybir
from concourse.bass_utils import run_bass_kernel_spmd

F32 = mybir.dt.float32
BF16 = mybir.dt.bfloat16
AF = mybir.ActivationFunctionType
ALU = mybir.AluOpType
AX = mybir.AxisListType

NB = 4
S = 2048
E = 1024
NH = 16
HD = 64
NI = 512
NEXP = 32
NKT = E // 128
NTT = S // 128
NPAIR = 8
GROUP = 1024
NG = S // GROUP
GT = GROUP // 128

ENGINES = ("sync", "gpsimd", "tensor", "scalar", "vector")


class Plan:

    def __init__(self):
        self.ops = []
        self.streams = {e: [] for e in ENGINES}
        self.spaces = {}
        self.dma_count = {"sync": 0, "gpsimd": 0}

    def op(self, engine, fn, r=(), w=(), dma=False, tag=""):
        idx = len(self.ops)
        rec = {"eng": engine, "fn": fn, "deps": set(), "inc": False,
               "dma": dma, "dma_i": (self.dma_count[engine] + 1) if dma else None,
               "tag": tag}
        if dma:
            self.dma_count[engine] += 1
        self.ops.append(rec)
        self.streams[engine].append(idx)

        def dep(p_idx, war_reader=False):
            if p_idx == idx:
                return
            p = self.ops[p_idx]
            if p["eng"] == engine and not p["dma"] and not dma:
                if engine == "tensor":
                    return
                if war_reader:
                    return
            rec["deps"].add(p_idx)

        for (sp, lo, hi) in r:
            s = self.spaces.setdefault(sp, {"w": [], "r": []})
            for (l2, h2, widx) in s["w"]:
                if lo < h2 and l2 < hi:
                    dep(widx)
            s["r"].append((lo, hi, idx))
        for (sp, lo, hi) in w:
            s = self.spaces.setdefault(sp, {"w": [], "r": []})
            for (l2, h2, widx) in s["w"]:
                if lo < h2 and l2 < hi:
                    dep(widx)
            for (l2, h2, ridx) in s["r"]:
                if lo < h2 and l2 < hi:
                    dep(ridx, war_reader=True)
            s["w"] = [t for t in s["w"] if not (lo <= t[0] and t[1] <= hi)]
            s["r"] = [t for t in s["r"] if not (lo <= t[0] and t[1] <= hi)]
            s["w"].append((lo, hi, idx))
        return idx

    def finalize(self):
        for rec in self.ops:
            for d in rec["deps"]:
                self.ops[d]["inc"] = True
        inc_no = {}
        counters = {e: 0 for e in ENGINES}
        for e in ENGINES:
            for idx in self.streams[e]:
                rec = self.ops[idx]
                if rec["dma"]:
                    inc_no[idx] = 16 * rec["dma_i"]
                elif rec["inc"]:
                    counters[e] += 1
                    inc_no[idx] = counters[e]
        self.inc_no = inc_no
        self.totals = counters
        self.dma_totals = {e: 16 * self.dma_count[e] for e in ("sync", "gpsimd")}

    def emit_stream(self, engine, eng, sems):
        max_waited = {}
        for idx in self.streams[engine]:
            rec = self.ops[idx]
            need = {}
            for d in rec["deps"]:
                p = self.ops[d]
                psem = ("dma_" + p["eng"]) if p["dma"] else p["eng"]
                v = self.inc_no[d]
                if v > need.get(psem, -1):
                    need[psem] = v
            for psem in sorted(need):
                v = need[psem]
                if max_waited.get(psem, 0) >= v:
                    continue
                eng.wait_ge(sems[psem], v)
                max_waited[psem] = v
            inst = rec["fn"](eng)
            if rec["dma"]:
                inst.then_inc(sems["dma_" + engine], 16)
            elif rec["inc"]:
                inst.then_inc(sems[engine], 1)


def build_program(n_batch=NB, dbg=()):
    nc = bass.Bass("TRN2", target_bir_lowering=False)

    d_x = nc.dram_tensor("x_in", [n_batch, S, E], F32, kind="ExternalInput").ap()
    d_wq = nc.dram_tensor("wq", [E, E], F32, kind="ExternalInput").ap()
    d_wk = nc.dram_tensor("wk", [E, E], F32, kind="ExternalInput").ap()
    d_wv = nc.dram_tensor("wv", [E, E], F32, kind="ExternalInput").ap()
    d_wo = nc.dram_tensor("wo", [E, E], F32, kind="ExternalInput").ap()
    d_wr = nc.dram_tensor("wr", [E, NEXP], F32, kind="ExternalInput").ap()
    d_br = nc.dram_tensor("br", [1, NEXP], F32, kind="ExternalInput").ap()
    d_wgs = nc.dram_tensor("wgs", [E, NI], F32, kind="ExternalInput").ap()
    d_bgs = nc.dram_tensor("bgs", [1, NI], F32, kind="ExternalInput").ap()
    d_wus = nc.dram_tensor("wus", [E, NI], F32, kind="ExternalInput").ap()
    d_bus = nc.dram_tensor("bus", [1, NI], F32, kind="ExternalInput").ap()
    d_wds = nc.dram_tensor("wds", [NI, E], F32, kind="ExternalInput").ap()
    d_bds = nc.dram_tensor("bds", [1, E], F32, kind="ExternalInput").ap()
    d_wg = nc.dram_tensor("wg", [NEXP, E, NI], F32, kind="ExternalInput").ap()
    d_bg = nc.dram_tensor("bg", [NEXP, NI], F32, kind="ExternalInput").ap()
    d_wu = nc.dram_tensor("wu", [NEXP, E, NI], F32, kind="ExternalInput").ap()
    d_bu = nc.dram_tensor("bu", [NEXP, NI], F32, kind="ExternalInput").ap()
    d_wd = nc.dram_tensor("wd", [NEXP, NI, E], F32, kind="ExternalInput").ap()
    d_bd = nc.dram_tensor("bd", [NEXP, E], F32, kind="ExternalInput").ap()
    d_ident = nc.dram_tensor("ident", [128, 128], F32, kind="ExternalInput").ap()
    d_out = nc.dram_tensor("y_out", [n_batch, S, E], F32, kind="ExternalOutput").ap()

    d_swqk = nc.dram_tensor("scr_wqk", [2, E, E], BF16, kind="Internal").ap()
    d_swg = nc.dram_tensor("scr_wg", [NEXP, E, NI], BF16, kind="Internal").ap()
    d_swu = nc.dram_tensor("scr_wu", [NEXP, E, NI], BF16, kind="Internal").ap()
    d_swd = nc.dram_tensor("scr_wd", [NEXP, NI, E], BF16, kind="Internal").ap()
    d_swgs = nc.dram_tensor("scr_wgs", [E, NI], BF16, kind="Internal").ap()
    d_swus = nc.dram_tensor("scr_wus", [E, NI], BF16, kind="Internal").ap()
    d_swds = nc.dram_tensor("scr_wds", [NI, E], BF16, kind="Internal").ap()

    dbg_out = {}
    for name, shape, dt in dbg:
        dbg_out[name] = nc.dram_tensor("dbg_" + name, list(shape), dt,
                                       kind="ExternalOutput").ap()

    from contextlib import ExitStack
    ctx = ExitStack()
    e_ = ctx.enter_context

    ar1 = e_(nc.sbuf_tensor("ar1", [128, 33280], mybir.dt.uint8))
    ar2 = e_(nc.sbuf_tensor("ar2", [128, 58368], mybir.dt.uint8))
    ar3 = e_(nc.sbuf_tensor("ar3", [128, 18432], mybir.dt.uint8))
    ar4 = e_(nc.sbuf_tensor("ar4", [128, 32768], mybir.dt.uint8))
    ar5 = e_(nc.sbuf_tensor("ar5", [128, 32768], mybir.dt.uint8))

    a1b, a2b = ar1.ap().bitcast(BF16), ar2.ap().bitcast(BF16)
    a2f = ar2.ap().bitcast(F32)
    a3b, a3f = ar3.ap().bitcast(BF16), ar3.ap().bitcast(F32)
    a4b = ar4.ap().bitcast(BF16)
    a5b, a5f = ar5.ap().bitcast(BF16), ar5.ap().bitcast(F32)

    t_vbuf = a1b
    t_y1nT = a1b

    def R_vbuf(h, tt):
        c = h * 1040 + tt * 65
        return ("ar1", 2 * c, 2 * (c + 65))

    def R_y1nT(kt, t0, t1):
        return ("ar1", 2 * (kt * 2048 + t0), 2 * (kt * 2048 + t1))

    def t_wqk(par, wi):
        return a2b[:, (par * 2 + wi) * 1024:(par * 2 + wi + 1) * 1024]

    def R_wqk(par, wi):
        return ("ar2", (par * 2 + wi) * 2048, (par * 2 + wi + 1) * 2048)

    t_ATr = [a2b[:, 4096 + i * 1024: 4096 + (i + 1) * 1024] for i in range(2)]

    def R_ATr(i):
        return ("ar2", 8192 + i * 2048, 8192 + (i + 1) * 2048)

    t_ustg = a2f[:, 3072:3584]
    R_ustg = ("ar2", 12288, 14336)
    t_rbc = a2f[:, 3584:4096]
    R_rbc = ("ar2", 14336, 16384)

    def t_qhs(par, i):
        return a2b[:, 8192 + (par * 4 + i) * 2048: 8192 + (par * 4 + i + 1) * 2048]

    def R_qhs(par, i):
        return ("ar2", 16384 + (par * 4 + i) * 4096, 16384 + (par * 4 + i + 1) * 4096)

    t_wo2 = [a2b[:, 8192 + j * 4096: 8192 + (j + 1) * 4096] for j in range(2)]

    def R_wo2(j):
        return ("ar2", 16384 + j * 8192, 16384 + (j + 1) * 8192)

    t_wv1 = a2b[:, 24576:28672]
    R_wv1 = ("ar2", 49152, 57344)

    t_wge = [a2b[:, i * 4096: (i + 1) * 4096] for i in range(2)]
    t_wue = [a2b[:, 4096 * 2 + i * 4096: 4096 * 2 + (i + 1) * 4096] for i in range(2)]
    t_wde = [a2b[:, 4096 * 4 + i * 4096: 4096 * 4 + (i + 1) * 4096] for i in range(2)]
    t_silu = a2b[:, 24576:26624]
    t_usb = a2b[:, 26624:28672]

    def R_wge(i):
        return ("ar2", i * 8192, (i + 1) * 8192)

    def R_wue(i):
        return ("ar2", 16384 + i * 8192, 16384 + (i + 1) * 8192)

    def R_wde(i):
        return ("ar2", 32768 + i * 8192, 32768 + (i + 1) * 8192)

    R_silu = ("ar2", 49152, 53248)
    R_usb = ("ar2", 53248, 57344)

    t_xroll = [a3b[:, i * 1024: (i + 1) * 1024] for i in range(2)]

    def R_xroll(i):
        return ("ar3", i * 2048, (i + 1) * 2048)

    t_projr = [a3b[:, i * 512: (i + 1) * 512] for i in range(2)]

    def R_projr(i):
        return ("ar3", i * 1024, (i + 1) * 1024)

    t_y1sc = [a3f[:, 512 + i * 1024: 512 + (i + 1) * 1024] for i in range(4)]

    def R_y1sc(i, lo=0, hi=1024):
        return ("ar3", 2048 + i * 4096 + 4 * lo, 2048 + i * 4096 + 4 * hi)

    t_y1scB = [a2f[:, i * 1024: (i + 1) * 1024] for i in range(4)]

    def R_y1scB(i, lo=0, hi=1024):
        return ("ar2", i * 4096 + 4 * lo, i * 4096 + 4 * hi)

    t_act = [a3b[:, i * 4096: (i + 1) * 4096] for i in range(2)]

    def R_act(i):
        return ("ar3", i * 8192, (i + 1) * 8192)

    t_xroll1 = a3b[:, 8192:9216]
    R_xroll1 = ("ar3", 16384, 18432)

    t_xT = a4b

    def R_xT(kt, t0, t1):
        return ("ar4", 2 * (kt * 2048 + t0), 2 * (kt * 2048 + t1))

    t_aoT = a5b

    def R_aoT(hp, t0, t1):
        return ("ar5", 2 * (hp * 2048 + t0), 2 * (hp * 2048 + t1))

    t_racc = a5f

    def R_racc(tsub):
        return ("ar5", tsub * 4096, (tsub + 1) * 4096)

    def sb(name, p, cols, dt):
        return e_(nc.sbuf_tensor(name, [p, cols], dt)).ap()

    t_ident = sb("identt", 128, 128, BF16)
    t_ones_r = sb("ones_r", 1, 128, BF16)
    t_oblk = sb("oblk", 128, 33, BF16)
    t_br = sb("brr", 1, NEXP, BF16)
    t_bds = sb("bdsr", 1, E, BF16)
    t_bd = sb("bdm", NEXP, E, BF16)
    t_bgT = sb("bgT", 128, 128, F32)
    t_buT = sb("buT", 128, 128, F32)
    t_bgsT = sb("bgsT", 128, 4, F32)
    t_busT = sb("busT", 128, 4, F32)
    t_wr = sb("wrm", 128, NKT * NEXP, BF16)
    t_eps = sb("epsc", 128, 1, F32)
    t_onecol = sb("onecol", 128, 1, F32)
    t_xf32 = [sb(f"xf32_{i}", 128, 1024, F32) for i in range(2)]
    t_y1nb = sb("y1nb", 128, 1024, BF16)
    t_lnt = t_y1nb
    t_means = sb("meanst", 128, 16, F32)
    t_negm2 = [sb(f"negm{i}", 128, 1, F32) for i in range(2)]
    t_vsum2 = [sb(f"vsum{i}", 128, 1, F32) for i in range(2)]
    t_rstd2 = [sb(f"rstd{i}", 128, 1, F32) for i in range(2)]
    t_aff = sb("aff", 128, NEXP, F32)
    t_top8 = sb("top8", 128, 8, F32)
    t_zap = sb("zap", 128, NEXP, F32)
    t_wmask = sb("wmask", 128, NEXP, F32)
    t_wsum = sb("wsum", 128, 1, F32)
    t_wrec = sb("wrec", 128, 1, F32)
    t_ws = sb("ws", 128, NTT * NEXP, F32)
    t_wT = sb("wTm", NEXP, S, BF16)
    t_urec = sb("urec", 128, 1, F32)
    t_outs = [sb(f"outs_{i}", 128, 1024, F32) for i in range(2)]
    t_wstage = sb("wstage", 128, 2 * NEXP, BF16)
    t_mscr = sb("mscr", 128, 1024, BF16)
    t_sqqs = [sb(f"sqq{i}", 128, 512, BF16) for i in range(2)]
    t_rr = sb("rr", 128, 512, BF16)
    t_ones64 = sb("ones64", 128, 64, BF16)

    def R_outs(blo, bhi):
        return ("outs", blo, bhi)

    def y1nbQ(tl):
        return t_outs[tl // 2].bitcast(BF16)[:, (tl % 2) * 1024:(tl % 2) * 1024 + 1024]

    def R_y1nbQ(tl, lo=0, hi=1024):
        base = (tl // 2) * 4096 + (tl % 2) * 2048
        return ("outs", base + 2 * lo, base + 2 * hi)

    def R(name):
        return (name, 0, 1 << 30)

    def Rws(tt, e0, e1):
        return ("ws", 4 * (tt * NEXP + e0), 4 * (tt * NEXP + e1))

    def RwT(t0, t1):
        return ("wT", 2 * t0, 2 * t1)

    psA = e_(nc.psum_tensor("psA", [128, 512], F32)).ap()
    psB = e_(nc.psum_tensor("psB", [128, 512], F32)).ap()
    psC = e_(nc.psum_tensor("psC", [128, 512], F32)).ap()
    psD = e_(nc.psum_tensor("psD", [128, 512], F32)).ap()
    psE = e_(nc.psum_tensor("psE", [128, 1024], F32)).ap()
    psF = e_(nc.psum_tensor("psF", [128, 1024], F32)).ap()
    ps1 = [psA, psB, psC, psD,
           psE[:, 0:512], psE[:, 512:1024], psF[:, 0:512], psF[:, 512:1024]]
    ps1b = [p.bitcast(BF16) for p in ps1]
    ps1r = [R("psA"), R("psB"), R("psC"), R("psD"),
            ("psE", 0, 1), ("psE", 1, 2), ("psF", 0, 1), ("psF", 1, 2)]
    R_psE = ("psE", 0, 2)
    R_psF = ("psF", 0, 2)

    P = Plan()

    def dma_g(dst, src, r=(), w=(), tag=""):
        P.op("gpsimd", lambda g: g.dma_start(dst, src), r=r, w=w, dma=True, tag=tag)

    def dma_s(dst, src, r=(), w=(), tag=""):
        P.op("sync", lambda s: s.dma_start(dst, src), r=r, w=w, dma=True, tag=tag)

    dma_g(t_ident[:, :], d_ident, [], [R("ident")], "ident")
    P.op("vector", lambda v: v.memset(t_ones_r[:, :], 1.0), w=[R("ones_r")])
    P.op("vector", lambda v: v.memset(t_oblk[:, :], 0.0), w=[R("oblk")])
    P.op("vector", lambda v: v.memset(t_oblk[0:64, 0:1], 1.0),
         r=[R("oblk")], w=[("oblk", 0, 0)])
    P.op("vector", lambda v: v.memset(t_oblk[64:128, 32:33], 1.0),
         r=[R("oblk")], w=[("oblk", 0, 0)])
    P.op("vector", lambda v: v.memset(t_eps[:, :], 1e-5), w=[R("eps")])
    P.op("vector", lambda v: v.memset(t_onecol[:, :], 1.0), w=[R("onecol")])
    P.op("vector", lambda v: v.memset(t_ones64[:, :], 1.0), w=[R("ones64")])
    P.op("vector", lambda v: v.memset(
        t_vbuf[:, 0:NH * 1040].rearrange("p (n c) -> p n c", c=65)[:, :, 64:65], 1.0),
        w=[("ar1", 0, 2 * NH * 1040)], tag="vones")

    dma_g(t_br[:, :], d_br, [], [R("br")], "br")
    dma_g(t_bds[:, :], d_bds, [], [R("bds")], "bds")
    dma_g(t_bd[:, :], d_bd, [], [R("bd")], "bd")
    dma_g(t_wr[:, :].rearrange("p (k e) -> p k e", e=NEXP),
          d_wr.rearrange("(k p) e -> p k e", p=128), [], [R("wr")], "wr")

    t_bstage = t_wT[:, 0:512]
    for nm, src, dstT in (("bg", d_bg, t_bgT), ("bu", d_bu, t_buT)):
        dma_g(t_bstage, src, [], [RwT(0, 512)], nm)
        for mt in range(4):
            pi = mt
            P.op("tensor", lambda t, mt=mt, pi=pi: t.transpose(
                ps1b[pi][:, 0:32], t_bstage[:, mt * 128:(mt + 1) * 128],
                t_ident[0:32, 0:32]),
                r=[RwT(0, 512), R("ident")], w=[ps1r[pi]])
            P.op("scalar", lambda s, mt=mt, pi=pi, dstT=dstT: s.copy(
                dstT.rearrange("p (e m) -> p e m", m=4)[:, :, mt:mt + 1],
                ps1b[pi][:, 0:32].rearrange("p (e m) -> p e m", m=1)),
                r=[ps1r[pi]], w=[(id(dstT).__str__(), 0, 1)], tag=nm + "T")

    def R_bT(dstT):
        return (id(dstT).__str__(), 0, 1)

    for nm, src, dstT in (("bgs", d_bgs, t_bgsT), ("bus", d_bus, t_busT)):
        dma_g(t_bstage[0:1, :], src, [], [RwT(0, 512)], nm)
        for mt in range(4):
            pi = mt
            P.op("tensor", lambda t, mt=mt, pi=pi: t.transpose(
                ps1b[pi][:, 0:1], t_bstage[0:1, mt * 128:(mt + 1) * 128],
                t_ident[0:1, 0:1]),
                r=[RwT(0, 512), R("ident")], w=[ps1r[pi]])
            P.op("scalar", lambda s, mt=mt, pi=pi, dstT=dstT: s.copy(
                dstT[:, mt:mt + 1], ps1b[pi][:, 0:1]),
                r=[ps1r[pi]], w=[R_bT(dstT)], tag=nm + "T")

    for wi, dsrc in ((0, d_wq), (1, d_wk)):
        for kt in range(NKT):
            st = t_xf32[kt % 2]
            skey = ("xf32", kt % 2, kt % 2 + 1)
            dma_g(st[:, :], dsrc[kt * 128:(kt + 1) * 128, :], [], [skey], "wqkld")
            P.op("vector", lambda v, st=st: v.tensor_reduce(
                t_means[:, :].rearrange("p (g o) -> p g o", o=1),
                st[:, :].rearrange("p (g c) -> p g c", c=64), AX.X, ALU.add),
                r=[skey], w=[R("means")])
            P.op("vector", lambda v: v.tensor_scalar(
                t_means[:, :], t_means[:, :], 1.0 / 64.0, None, ALU.mult),
                r=[R("means")], w=[R("means")])
            for g in range(16):
                P.op("vector", lambda v, st=st, g=g: v.tensor_scalar(
                    t_lnt[:, g * 64:(g + 1) * 64], st[:, g * 64:(g + 1) * 64],
                    t_means[:, g:g + 1], None, ALU.subtract),
                    r=[skey, R("means")], w=[("y1nb", g, g + 1)])
            dma_s(d_swqk[wi, kt * 128:(kt + 1) * 128, :], t_lnt[:, :],
                  [("y1nb", 0, 16)], [("swqk", wi * 8 + kt, wi * 8 + kt + 1)], "wqkst")

    def emit_A_tile(bn, tt, xr, rx):
        dma_g(xr[:, :], d_x[bn, tt * 128:(tt + 1) * 128, :], [], [rx], "xld")
        for et in range(NKT):
            pi = (tt * NKT + et) % 4
            P.op("tensor", lambda t, xr=xr, et=et, pi=pi: t.transpose(
                ps1b[pi][:, 0:128], xr[:, et * 128:(et + 1) * 128], t_ident),
                r=[rx, R("ident")], w=[ps1r[pi]], tag="xT")
            P.op("scalar", lambda s, et=et, tt=tt, pi=pi: s.copy(
                t_xT[:, et * 2048 + tt * 128: et * 2048 + (tt + 1) * 128],
                ps1b[pi][:, 0:128]),
                r=[ps1r[pi]], w=[R_xT(et, tt * 128, (tt + 1) * 128)])

    pending_finish = []
    for b in range(n_batch):
        if b == 0:
            for tt in range(NTT):
                emit_A_tile(0, tt, t_xroll[tt % 2], R_xroll(tt % 2))

        for par in range(2):
            for i in range(4):
                P.op("vector", lambda v, par=par, i=i: v.memset(
                    t_qhs(par, i)[64:128, :], 0.0),
                    w=[R_qhs(par, i)], tag="qhz")

        def c1_maker(p):
            par = p % 2
            pending = []
            state = {"j": 0}

            def flush():
                while pending:
                    pending.pop(0)()

            def emit_next():
                if state["j"] < 8:
                    emit_slice(state["j"])
                    state["j"] += 1

            def emit_slice(j):
                flush()
                sc, wi = j >> 1, j & 1
                wt, rwt = t_wqk(par, wi), R_wqk(par, wi)
                if j <= 1:
                    dma_s(wt[:, :].rearrange("p (k m) -> p k m", m=128),
                          d_swqk[wi].rearrange("(k p) m -> p k m", p=128)
                          [:, :, p * 128:(p + 1) * 128],
                          [("swqk", wi * 8, wi * 8 + 8)], [rwt], "wqkpld")
                ps, rps = (psC, R("psC")) if wi == 0 else (psD, R("psD"))
                for kt in range(NKT):
                    P.op("tensor", lambda t, ps=ps, wt=wt, kt=kt, sc=sc: t.matmul(
                        ps[:, :], wt[:, kt * 128:(kt + 1) * 128],
                        t_xT[:, kt * 2048 + sc * 512: kt * 2048 + (sc + 1) * 512],
                        start=(kt == 0), stop=(kt == NKT - 1)),
                        r=[R_xT(kt, sc * 512, (sc + 1) * 512), rwt], w=[rps],
                        tag="qkmm")
                sqq, rsqq = t_sqqs[wi], R(f"sqq{wi}")
                qa, ra = t_qhs(par, 2 * wi + 0), R_qhs(par, 2 * wi + 0)
                qb, rb = t_qhs(par, 2 * wi + 1), R_qhs(par, 2 * wi + 1)
                P.op("scalar", lambda s, ps=ps, sqq=sqq: s.activation(
                    sqq[:, :], ps[:, :], AF.Square),
                    r=[rps], w=[rsqq], tag="sq")
                P.op("scalar", lambda s, ps=ps, qa=qa, sc=sc: s.copy(
                    qa[0:64, sc * 512:(sc + 1) * 512], ps[0:64, :]),
                    r=[rps], w=[ra], tag="qev")
                P.op("scalar", lambda s, ps=ps, qb=qb, sc=sc: s.copy(
                    qb[0:64, sc * 512:(sc + 1) * 512], ps[64:128, :]),
                    r=[rps], w=[rb], tag="qev")

                def var(ps=ps, rps=rps, sqq=sqq, rsqq=rsqq, qa=qa, ra=ra,
                        qb=qb, rb=rb, sc=sc):
                    P.op("tensor", lambda t: t.matmul(
                        ps[0:33, :], t_oblk[:, :], sqq[:, :], start=True, stop=True),
                        r=[rsqq, R("oblk")], w=[rps], tag="varmm")
                    P.op("scalar", lambda s: s.activation(
                        qa[64:65, sc * 512:(sc + 1) * 512], ps[0:1, :], AF.Copy,
                        scale=1.0 / 63.0),
                        r=[rps], w=[ra], tag="varev")
                    P.op("scalar", lambda s: s.activation(
                        qb[64:65, sc * 512:(sc + 1) * 512], ps[32:33, :], AF.Copy,
                        scale=1.0 / 63.0),
                        r=[rps], w=[rb], tag="varev")
                pending.append(var)
            return emit_next, flush

        c1 = [c1_maker(p) for p in range(NPAIR)]

        dma_g(t_wv1[:, :].rearrange("p (k i) -> p k i", i=512),
              d_wv.rearrange("(k p) m -> p k m", p=128)[:, :, 0:512],
              [], [R_wv1], "wvld")
        for j in range(8):
            c1[0][0]()
            if pending_finish:
                pending_finish.pop(0)()
        assert not pending_finish
        P.op("vector", lambda v: v.memset(
            t_vbuf[:, 0:NH * 1040].rearrange("p (n c) -> p n c", c=65)[:, :, 64:65], 1.0),
            w=[("ar1", 0, 2 * NH * 1040)], tag="vones")
        for ec in range(2):
            if ec == 1:
                dma_g(t_wv1[:, :].rearrange("p (k i) -> p k i", i=512),
                      d_wv.rearrange("(k p) m -> p k m", p=128)[:, :, 512:1024],
                      [], [R_wv1], "wvld")
                for _ in range(3):
                    c1[1][0]()
            for tt in range(NTT):
                ps, rps = ps1[tt % 2], ps1r[tt % 2]
                for kt in range(NKT):
                    P.op("tensor", lambda t, ps=ps, kt=kt, tt=tt: t.matmul(
                        ps[:, :], t_xT[:, kt * 2048 + tt * 128: kt * 2048 + (tt + 1) * 128],
                        t_wv1[:, kt * 512:(kt + 1) * 512],
                        start=(kt == 0), stop=(kt == NKT - 1)),
                        r=[R_xT(kt, tt * 128, (tt + 1) * 128), R_wv1], w=[rps], tag="vmm")
                P.op("vector", lambda v, ps=ps, ec=ec, tt=tt: v.tensor_copy(
                    t_vbuf.rearrange("p (j c) -> p j c", c=1040)
                    [:, ec * 8:(ec + 1) * 8, tt * 65: tt * 65 + 64],
                    ps[:, :].rearrange("p (j c) -> p j c", c=64)),
                    r=[rps], w=[R_vbuf(ec * 8 + j, tt) for j in range(8)],
                    tag="vev")

        def emit_umm(h, tp):
            at = t_ATr[tp % 2]
            for half in range(2):
                tt = 2 * tp + half
                P.op("tensor", lambda t, h=h, tt=tt, at=at, half=half: t.matmul(
                    psA[0:65, 0:512],
                    t_vbuf[:, h * 1040 + tt * 65: h * 1040 + (tt + 1) * 65],
                    at[:, half * 512:(half + 1) * 512],
                    start=(tt == 0), stop=(tt == NTT - 1)),
                    r=[R_vbuf(h, tt), R_ATr(tp % 2)], w=[ps1r[0]], tag="ummv")

        def emit_c2_step(hp, hl, sc):
            par = hp % 2
            h = hp * 2 + hl
            qh, rqh = t_qhs(par, hl), R_qhs(par, hl)
            kh, rkh = t_qhs(par, 2 + hl), R_qhs(par, 2 + hl)
            for tp in range(8):
                wps = psE if tp % 2 == 0 else psF
                base = 4 if tp % 2 == 0 else 6
                for half in range(2):
                    tt = 2 * tp + half
                    P.op("tensor", lambda t, wps=wps, half=half, kh=kh, qh=qh,
                         tt=tt, sc=sc: t.matmul(
                             wps[:, half * 512:(half + 1) * 512],
                             kh[:, tt * 128:(tt + 1) * 128],
                             qh[:, sc * 512:(sc + 1) * 512], start=True, stop=True),
                        r=[rkh, rqh], w=[ps1r[base + half]], tag="smm")
                if tp >= 2:
                    emit_umm(h, tp - 2)
                at = t_ATr[tp % 2]
                P.op("scalar", lambda s, at=at, wps=wps: s.activation(
                    at[:, :], wps[:, 0:1024], AF.Exp, scale=1.0 / 64.0),
                    r=[ps1r[base], ps1r[base + 1]], w=[R_ATr(tp % 2)], tag="exp")
            emit_umm(h, 6)
            emit_umm(h, 7)
            if hl == 0:
                P.op("scalar", lambda s: s.copy(
                    t_ustg[0:64, :], psA[0:64, 0:512]),
                    r=[ps1r[0]], w=[R_ustg], tag="ucp")
            else:
                P.op("scalar", lambda s: s.copy(
                    t_ustg[64:128, :], psA[0:64, 0:512]),
                    r=[ps1r[0]], w=[R_ustg], tag="ucp")
            P.op("scalar", lambda s: s.activation(
                t_xf32[0][64:65, 0:512], psA[64:65, 0:512], AF.Ln),
                r=[ps1r[0]], w=[("xf32", 0, 1)], tag="rln")
            P.op("scalar", lambda s: s.activation(
                t_rr[64:65, 0:512], t_xf32[0][64:65, 0:512], AF.Exp, scale=-1.0),
                r=[("xf32", 0, 1)], w=[R("rr")], tag="rrec")

            def tail(hl=hl, hp=hp, sc=sc):
                P.op("tensor", lambda t: t.matmul(
                    psB[hl * 64:(hl + 1) * 64, 0:512], t_ones64[64:65, 0:64],
                    t_rr[64:65, 0:512], start=True, stop=True),
                    r=[R("ones64"), R("rr")], w=[R("psB")], tag="bcmm")
                P.op("scalar", lambda s: s.copy(
                    t_rbc[hl * 64:(hl + 1) * 64, :], psB[hl * 64:(hl + 1) * 64, :]),
                    r=[R("psB")], w=[R_rbc], tag="rbccp")
                P.op("vector", lambda v: v.tensor_tensor(
                    t_aoT[hl * 64:(hl + 1) * 64, hp * 2048 + sc * 512: hp * 2048 + (sc + 1) * 512],
                    t_ustg[hl * 64:(hl + 1) * 64, :], t_rbc[hl * 64:(hl + 1) * 64, :],
                    ALU.mult),
                    r=[R_ustg, R_rbc], w=[R_aoT(hp, sc * 512, (sc + 1) * 512)],
                    tag="aomul")
            return tail

        for hp in range(NPAIR):
            c1[hp][1]()
            for hl in range(2):
                for sc in range(4):
                    tail = emit_c2_step(hp, hl, sc)
                    if hp + 1 < NPAIR:
                        c1[hp + 1][0]()
                    tail()
            if hp == 6:
                for jw in range(2):
                    dma_g(t_wo2[jw][:, :].rearrange("p (k i) -> p k i", i=512),
                          d_wo.rearrange("(k p) m -> p k m", p=128)
                          [:, :, jw * 512:(jw + 1) * 512],
                          [], [R_wo2(jw)], "wold")


        def emit_router(tc):
            pi = (2 * (tc % 4)) % 4
            pr, rpr = ps1[pi], ps1r[pi]
            for kt in range(NKT):
                P.op("tensor", lambda t, kt=kt, tc=tc, pr=pr: t.matmul(
                    pr[:, 0:NEXP],
                    t_y1nT[:, kt * 2048 + tc * 128: kt * 2048 + (tc + 1) * 128],
                    t_wr[:, kt * NEXP:(kt + 1) * NEXP],
                    start=(kt == 0), stop=False),
                    r=[R_y1nT(kt, tc * 128, (tc + 1) * 128), R("wr")], w=[rpr],
                    tag="rmm")
            P.op("tensor", lambda t, pr=pr: t.matmul(
                pr[:, 0:NEXP], t_ones_r[:, :], t_br[:, :], start=False, stop=True),
                r=[R("ones_r"), R("br")], w=[rpr], tag="rbias")
            P.op("scalar", lambda s, pr=pr: s.activation(
                t_aff[:, :], pr[:, 0:NEXP], AF.Sigmoid),
                r=[rpr], w=[R("aff")], tag="sig")
            P.op("vector", lambda v: v.max(out=t_top8[:, :], in_=t_aff[:, :]),
                 r=[R("aff")], w=[R("top8")])
            P.op("vector", lambda v: v.match_replace(
                out=t_zap[:, :], in_to_replace=t_top8[:, :], in_values=t_aff[:, :],
                imm_value=0.0),
                r=[R("aff"), R("top8")], w=[R("zap"), R("top8")])
            P.op("vector", lambda v: v.tensor_sub(
                t_wmask[:, :], t_aff[:, :], t_zap[:, :]),
                r=[R("aff"), R("zap")], w=[R("wmask")])
            P.op("vector", lambda v: v.tensor_reduce(
                t_wsum[:, :], t_top8[:, :], AX.X, ALU.add),
                r=[R("top8")], w=[R("wsum")])
            P.op("vector", lambda v: v.tensor_scalar(
                t_wsum[:, :], t_wsum[:, :], 1e-9, None, ALU.add),
                r=[R("wsum")], w=[R("wsum")])
            P.op("vector", lambda v: v.reciprocal(t_wrec[:, :], t_wsum[:, :]),
                 r=[R("wsum")], w=[R("wrec")])
            P.op("vector", lambda v, tc=tc: v.tensor_scalar(
                t_ws[:, tc * NEXP:(tc + 1) * NEXP], t_wmask[:, :], t_wrec[:, :],
                None, ALU.mult),
                r=[R("wmask"), R("wrec")], w=[Rws(tc, 0, NEXP)], tag="wnorm")
            sl = tc % 2
            P.op("vector", lambda v, tc=tc, sl=sl: v.tensor_copy(
                t_wstage[:, sl * NEXP:(sl + 1) * NEXP],
                t_ws[:, tc * NEXP:(tc + 1) * NEXP]),
                r=[Rws(tc, 0, NEXP)], w=[("wstage", sl, sl + 1)], tag="wbf")

        def emit_wT(tc):
            sl = tc % 2
            pj = (2 * (tc % 4) + 1) % 4
            P.op("tensor", lambda t, sl=sl, pj=pj: t.transpose(
                ps1b[pj][0:NEXP, 0:128], t_wstage[:, sl * NEXP:(sl + 1) * NEXP],
                t_ident),
                r=[("wstage", sl, sl + 1), R("ident")], w=[ps1r[pj]], tag="wTmm")
            P.op("scalar", lambda s, tc=tc, pj=pj: s.copy(
                t_wT[:, tc * 128:(tc + 1) * 128], ps1b[pj][0:NEXP, 0:128]),
                r=[ps1r[pj]], w=[RwT(tc * 128, (tc + 1) * 128)], tag="wTev")

        def emit_deferred(tc):
            if tc >= 1:
                emit_wT(tc - 1)
            tl = tc % 4
            for et in range(NKT):
                pj = 4 + et % 4
                P.op("tensor", lambda t, et=et, tl=tl, pj=pj: t.transpose(
                    ps1b[pj][:, 0:128], y1nbQ(tl)[:, et * 128:(et + 1) * 128],
                    t_ident),
                    r=[R_y1nbQ(tl, et * 128, (et + 1) * 128), R("ident")],
                    w=[ps1r[pj]], tag="y1T")
                P.op("scalar", lambda s, et=et, tc=tc, pj=pj: s.copy(
                    t_y1nT[:, et * 2048 + tc * 128: et * 2048 + (tc + 1) * 128],
                    ps1b[pj][:, 0:128]),
                    r=[ps1r[pj]], w=[R_y1nT(et, tc * 128, (tc + 1) * 128)],
                    tag="y1Tev")
            emit_router(tc)
            if tc == NTT - 1:
                emit_wT(tc)

        dma_s(t_xf32[0][:, :], d_x[b, 0:128, :], [], [("xf32", 0, 1)], "xres")
        d4flush = [(lambda tl=tl: emit_deferred(12 + tl)) for tl in range(4)]
        for sc in range(4):
            for tl in range(4):
                if sc > 0:
                    emit_deferred((sc - 1) * 4 + tl)
                if sc < 4:
                    tt = sc * 4 + tl
                    if tt + 1 < NTT:
                        p2n = (tt + 1) % 2
                        dma_s(t_xf32[p2n][:, :], d_x[b, (tt + 1) * 128:(tt + 2) * 128, :],
                              [], [("xf32", p2n, p2n + 1)], "xres")
                    ysel, Rysel = ((t_y1sc, R_y1sc) if sc % 2 == 1
                                   else (t_y1scB, R_y1scB))
                    y1 = ysel[tl]
                    ry1 = Rysel(tl)
                    rx = ("xf32", tt % 2, tt % 2 + 1)
                    for half in range(2):
                        pi = (2 * tl + half) % 4
                        ps, rps = ps1[pi], ps1r[pi]
                        wvb, rwv = t_wo2[half], R_wo2(half)
                        for kt in range(NKT):
                            P.op("tensor", lambda t, ps=ps, kt=kt, tt=tt, wvb=wvb: t.matmul(
                                ps[:, :],
                                t_aoT[:, kt * 2048 + tt * 128: kt * 2048 + (tt + 1) * 128],
                                wvb[:, kt * 512:(kt + 1) * 512],
                                start=(kt == 0), stop=(kt == NKT - 1)),
                                r=[R_aoT(kt, tt * 128, (tt + 1) * 128), rwv],
                                w=[rps], tag="pmm")
                        P.op("vector", lambda v, ps=ps, y1=y1, half=half,
                             xr=t_xf32[tt % 2]: v.tensor_add(
                                 y1[:, half * 512:(half + 1) * 512], ps[:, :],
                                 xr[:, half * 512:(half + 1) * 512]),
                             r=[rps, rx], w=[Rysel(tl, half * 512, (half + 1) * 512)],
                             tag="resadd")
                    p2 = tl % 2
                    negm, vsum, rstd = t_negm2[p2], t_vsum2[p2], t_rstd2[p2]
                    rng, rvs, rrs = (f"negm{p2}", 0, 1), (f"vsum{p2}", 0, 1), (f"rstd{p2}", 0, 1)
                    P.op("vector", lambda v, y1=y1, negm=negm: v.tensor_scalar(
                        t_mscr[:, :], y1[:, :], -1.0 / 1024.0, None,
                        ALU.mult, ALU.add, accum_out=negm[:, :]),
                        r=[ry1], w=[R("mscr"), rng], tag="lnmean")
                    P.op("scalar", lambda s, y1=y1, negm=negm, vsum=vsum: s.activation(
                        t_y1nb[:, :], y1[:, :], AF.Square, bias=negm[:, :],
                        accum_out=vsum[:, :]),
                        r=[ry1, rng], w=[R("y1nb"), rvs], tag="lnsq")
                    P.op("scalar", lambda s, vsum=vsum, rstd=rstd: s.activation(
                        rstd[:, :], vsum[:, :], AF.Sqrt, bias=t_eps[:, :],
                        scale=1.0 / 1024.0),
                        r=[rvs, R("eps")], w=[rrs], tag="lnsqrt")
                    P.op("vector", lambda v, rstd=rstd: v.reciprocal(rstd[:, :], rstd[:, :]),
                         r=[rrs], w=[rrs])
                    P.op("vector", lambda v, y1=y1, negm=negm, rstd=rstd, tl=tl: v.tensor_scalar(
                        y1nbQ(tl)[:, :], y1[:, :], negm[:, :], rstd[:, :],
                        ALU.add, ALU.mult),
                        r=[ry1, rng, rrs], w=[R_y1nbQ(tl)], tag="ln1n")

        a_next = ([(lambda tt=tt: emit_A_tile(b + 1, tt, t_xroll1, R_xroll1))
                   for tt in range(NTT)] if b + 1 < n_batch else [])
        for g in range(NG):
            first = (b == 0 and g == 0)

            def wsrc(ve):
                if ve == 0:
                    return (d_wgs, d_wus, d_wds, d_swgs, d_swus, d_swds, NEXP * 3)
                e = ve - 1
                return (d_wg[e], d_wu[e], d_wd[e], d_swg[e], d_swu[e], d_swd[e],
                        e * 3)

            def plan_expert_dma(ve):
                sl = ve % 2
                sg_, su_, sd_, cg_, cu_, cd_, sx = wsrc(ve)
                if first:
                    dma_g(t_wge[sl][:, :].rearrange("p (k i) -> p k i", i=512),
                          sg_.rearrange("(k p) i -> p k i", p=128),
                          [], [R_wge(sl)], "wgld")
                    dma_s(cg_.rearrange("(k p) i -> p k i", p=128),
                          t_wge[sl][:, :].rearrange("p (k i) -> p k i", i=512),
                          [R_wge(sl)], [("sexp", sx, sx + 1)], "wgst")
                    dma_g(t_wue[sl][:, :].rearrange("p (k i) -> p k i", i=512),
                          su_.rearrange("(k p) i -> p k i", p=128),
                          [], [R_wue(sl)], "wuld")
                    dma_s(cu_.rearrange("(k p) i -> p k i", p=128),
                          t_wue[sl][:, :].rearrange("p (k i) -> p k i", i=512),
                          [R_wue(sl)], [("sexp", sx + 1, sx + 2)], "wust")
                    dma_g(t_wde[sl][:, :].rearrange("p (k m) -> p k m", m=1024),
                          sd_.rearrange("(k p) m -> p k m", p=128),
                          [], [R_wde(sl)], "wdld")
                    dma_s(cd_.rearrange("(k p) m -> p k m", p=128),
                          t_wde[sl][:, :].rearrange("p (k m) -> p k m", m=1024),
                          [R_wde(sl)], [("sexp", sx + 2, sx + 3)], "wdst")
                else:
                    dma_g(t_wge[sl][:, :].rearrange("p (k i) -> p k i", i=512),
                          cg_.rearrange("(k p) i -> p k i", p=128),
                          [("sexp", sx, sx + 1)], [R_wge(sl)], "wgld")
                    dma_g(t_wue[sl][:, :].rearrange("p (k i) -> p k i", i=512),
                          cu_.rearrange("(k p) i -> p k i", p=128),
                          [("sexp", sx + 1, sx + 2)], [R_wue(sl)], "wuld")
                    dma_g(t_wde[sl][:, :].rearrange("p (k m) -> p k m", m=1024),
                          cd_.rearrange("(k p) m -> p k m", p=128),
                          [("sexp", sx + 2, sx + 3)], [R_wde(sl)], "wdld")

            def emit_gu(ve, filler=None):
                sl = ve % 2
                wgt, rwg = t_wge[sl], R_wge(sl)
                wut, rwu = t_wue[sl], R_wue(sl)
                if ve == 0:
                    bgc, buc = t_bgsT, t_busT
                    bge = bue = 0
                else:
                    bgc, buc = t_bgT, t_buT
                    bge = bue = (ve - 1) * 4
                act = t_act[ve % 2]
                ract = R_act(ve % 2)
                for sub in range(2):
                    t0 = g * GROUP + sub * 512
                    for mt in range(4):
                        ps, rps = ps1[mt % 2], ps1r[mt % 2]
                        ps2, rps2 = ps1[2 + mt % 2], ps1r[2 + mt % 2]
                        for kt in range(NKT):
                            P.op("tensor", lambda t, ps=ps, wgt=wgt, kt=kt, mt=mt, t0=t0: t.matmul(
                                ps[:, :], wgt[:, kt * 512 + mt * 128: kt * 512 + (mt + 1) * 128],
                                t_y1nT[:, kt * 2048 + t0: kt * 2048 + t0 + 512],
                                start=(kt == 0), stop=(kt == NKT - 1)),
                                r=[R_y1nT(kt, t0, t0 + 512), rwg], w=[rps], tag="gmm")
                        for kt in range(NKT):
                            P.op("tensor", lambda t, ps2=ps2, wut=wut, kt=kt, mt=mt, t0=t0: t.matmul(
                                ps2[:, :], wut[:, kt * 512 + mt * 128: kt * 512 + (mt + 1) * 128],
                                t_y1nT[:, kt * 2048 + t0: kt * 2048 + t0 + 512],
                                start=(kt == 0), stop=(kt == NKT - 1)),
                                r=[R_y1nT(kt, t0, t0 + 512), rwu], w=[rps2], tag="umm2")
                        P.op("scalar", lambda s, ps=ps, mt=mt, bgc=bgc, bge=bge: s.activation(
                            t_silu[:, mt * 512:(mt + 1) * 512], ps[:, :], AF.Silu,
                            bias=bgc[:, bge + mt: bge + mt + 1]),
                            r=[rps, R_bT(bgc)], w=[("ar2", 49152 + mt * 1024, 49152 + (mt + 1) * 1024)],
                            tag="silu")
                        P.op("vector", lambda v, ps2=ps2, act=act, sub=sub, mt=mt,
                             buc=buc, bue=bue: v.scalar_tensor_tensor(
                                 act[:, sub * 2048 + mt * 512: sub * 2048 + (mt + 1) * 512],
                                 ps2[:, :], buc[:, bue + mt: bue + mt + 1],
                                 t_silu[:, mt * 512:(mt + 1) * 512],
                                 ALU.add, ALU.mult),
                             r=[rps2, R_bT(buc),
                                ("ar2", 49152 + mt * 1024, 49152 + (mt + 1) * 1024)],
                             w=[("ar3", (ve % 2) * 8192 + sub * 4096 + mt * 1024,
                                (ve % 2) * 8192 + sub * 4096 + (mt + 1) * 1024)],
                             tag="actm")
                        if filler:
                            filler.pop(0)()

            def emit_d(vd):
                wdt, rwd = t_wde[vd % 2], R_wde(vd % 2)
                act = t_act[vd % 2]
                ract = R_act(vd % 2)
                for tsub in range(GT):
                    sub, tl0 = tsub // 4, (tsub % 4) * 128
                    psd = psE if tsub % 2 == 0 else psF
                    rpsd = R_psE if tsub % 2 == 0 else R_psF
                    for it in range(4):
                        for hc in range(2):
                            P.op("tensor", lambda t, psd=psd, act=act, wdt=wdt,
                                 it=it, hc=hc, sub=sub, tl0=tl0: t.matmul(
                                     psd[:, hc * 512:(hc + 1) * 512],
                                     act[:, sub * 2048 + it * 512 + tl0: sub * 2048 + it * 512 + tl0 + 128],
                                     wdt[:, it * 1024 + hc * 512: it * 1024 + (hc + 1) * 512],
                                     start=(it == 0), stop=(it == 3)),
                                r=[ract, rwd], w=[rpsd], tag="dmm")
                    tt_g = g * GT + tsub
                    if vd == 0:
                        P.op("scalar", lambda s, psd=psd, tsub=tsub: s.copy(
                            t_racc[:, tsub * 1024:(tsub + 1) * 1024], psd[:, :]),
                            r=[rpsd], w=[R_racc(tsub)], tag="dcp")
                    else:
                        e = vd - 1
                        P.op("vector", lambda v, psd=psd, tsub=tsub, tt_g=tt_g, e=e:
                             v.scalar_tensor_tensor(
                                 t_racc[:, tsub * 1024:(tsub + 1) * 1024], psd[:, :],
                                 t_ws[:, tt_g * NEXP + e: tt_g * NEXP + e + 1],
                                 t_racc[:, tsub * 1024:(tsub + 1) * 1024],
                                 ALU.mult, ALU.add),
                             r=[rpsd, Rws(tt_g, e, e + 1), R_racc(tsub)],
                             w=[R_racc(tsub)], tag="dstt")

            def mk_finish(g, tsub, b=b):
                def go():
                    tt_g = g * GT + tsub
                    psd = psE if tsub % 2 == 0 else psF
                    rpsd = R_psE if tsub % 2 == 0 else R_psF
                    for hc in range(2):
                        P.op("tensor", lambda t, psd=psd, hc=hc, tt_g=tt_g: t.matmul(
                            psd[:, hc * 512:(hc + 1) * 512],
                            t_wT[:, tt_g * 128:(tt_g + 1) * 128],
                            t_bd[:, hc * 512:(hc + 1) * 512], start=True, stop=False),
                            r=[RwT(tt_g * 128, (tt_g + 1) * 128), R("bd")], w=[rpsd],
                            tag="bmm")
                        P.op("tensor", lambda t, psd=psd, hc=hc: t.matmul(
                            psd[:, hc * 512:(hc + 1) * 512], t_ones_r[:, :],
                            t_bds[:, hc * 512:(hc + 1) * 512], start=False, stop=False),
                            r=[R("ones_r"), R("bds")], w=[rpsd], tag="bmm2")
                        for et in range(4):
                            ec4 = hc * 4 + et
                            P.op("tensor", lambda t, psd=psd, hc=hc, et=et, ec4=ec4,
                                 tt_g=tt_g: t.matmul(
                                     psd[:, hc * 512 + et * 128: hc * 512 + (et + 1) * 128],
                                     t_y1nT[:, ec4 * 2048 + tt_g * 128: ec4 * 2048 + (tt_g + 1) * 128],
                                     t_ident, start=False, stop=(et == 3)),
                                r=[R_y1nT(ec4, tt_g * 128, (tt_g + 1) * 128), R("ident")],
                                w=[rpsd], tag="ymm")
                    P.op("vector", lambda v, psd=psd, tsub=tsub: v.tensor_add(
                        t_racc[:, tsub * 1024:(tsub + 1) * 1024], psd[:, :],
                        t_racc[:, tsub * 1024:(tsub + 1) * 1024]),
                        r=[rpsd, R_racc(tsub)], w=[R_racc(tsub)], tag="badd")
                    p2 = tsub % 2
                    oy = t_outs[p2]
                    roy = R_outs(p2 * 4096, (p2 + 1) * 4096)
                    negm, vsum, rstd = t_negm2[p2], t_vsum2[p2], t_rstd2[p2]
                    rng, rvs, rrs = (f"negm{p2}", 0, 1), (f"vsum{p2}", 0, 1), (f"rstd{p2}", 0, 1)
                    P.op("vector", lambda v, tsub=tsub, negm=negm: v.tensor_scalar(
                        t_mscr[:, :], t_racc[:, tsub * 1024:(tsub + 1) * 1024],
                        -1.0 / 1024.0, None, ALU.mult, ALU.add, accum_out=negm[:, :]),
                        r=[R_racc(tsub)], w=[R("mscr"), rng], tag="ln2mean")
                    P.op("scalar", lambda s, tsub=tsub, negm=negm, vsum=vsum: s.activation(
                        t_y1nb[:, :], t_racc[:, tsub * 1024:(tsub + 1) * 1024],
                        AF.Square, bias=negm[:, :], accum_out=vsum[:, :]),
                        r=[R_racc(tsub), rng], w=[R("y1nb"), rvs], tag="ln2sq")
                    P.op("scalar", lambda s, vsum=vsum, rstd=rstd: s.activation(
                        rstd[:, :], vsum[:, :], AF.Sqrt, bias=t_eps[:, :],
                        scale=1.0 / 1024.0),
                        r=[rvs, R("eps")], w=[rrs], tag="ln2sqrt")
                    P.op("vector", lambda v, rstd=rstd: v.reciprocal(rstd[:, :], rstd[:, :]),
                         r=[rrs], w=[rrs])
                    P.op("vector", lambda v, oy=oy, tsub=tsub, negm=negm, rstd=rstd: v.tensor_scalar(
                        oy[:, :], t_racc[:, tsub * 1024:(tsub + 1) * 1024],
                        negm[:, :], rstd[:, :], ALU.add, ALU.mult),
                        r=[R_racc(tsub), rng, rrs], w=[roy], tag="ln2n")
                    dma_s(d_out[b, tt_g * 128:(tt_g + 1) * 128, :], oy[:, :],
                          [roy], [], "outst")
                return go

            plan_expert_dma(0)
            plan_expert_dma(1)
            filler = d4flush if g == 0 else pending_finish
            for ve in range(NEXP + 2):
                if ve <= NEXP:
                    emit_gu(ve, filler if ve == 0 else None)
                if ve >= 1:
                    emit_d(ve - 1)
                if 1 <= ve < NEXP:
                    plan_expert_dma(ve + 1)
                if g == 1 and a_next and ve % 2 == 0:
                    a_next.pop(0)()
            pending_finish = [mk_finish(g, tsub) for tsub in range(GT)]

        if b == n_batch - 1:
            for go in pending_finish:
                go()
            pending_finish = []

        if b == 0:
            if "xT" in dbg_out:
                dma_s(dbg_out["xT"], t_xT[:, :], [R_xT(kt, 0, S) for kt in range(NKT)], [])
            if "aoT" in dbg_out:
                dma_s(dbg_out["aoT"], t_aoT[:, :], [R_aoT(hp, 0, S) for hp in range(NPAIR)], [])
            if "y1nT" in dbg_out:
                dma_s(dbg_out["y1nT"], t_y1nT[:, 0:NKT * S],
                      [R_y1nT(kt, 0, S) for kt in range(NKT)], [])
            if "ws" in dbg_out:
                dma_s(dbg_out["ws"], t_ws[:, :], [Rws(tt, 0, NEXP) for tt in range(NTT)], [])

    P.finalize()
    sems = {}
    block = None
    sem_objs = []
    for nm in ("tensor", "scalar", "vector", "gpsimd", "sync", "dma_gpsimd", "dma_sync"):
        h = ctx.enter_context(nc.semaphore("sem_" + nm))
        sems[nm] = h
    block = ctx.enter_context(nc.Block())

    @block.sync
    def _(sy):
        P.emit_stream("sync", sy, sems)
        if P.dma_totals["sync"]:
            sy.wait_ge(sems["dma_sync"], P.dma_totals["sync"])

    @block.gpsimd
    def _(gp):
        P.emit_stream("gpsimd", gp, sems)
        if P.dma_totals["gpsimd"]:
            gp.wait_ge(sems["dma_gpsimd"], P.dma_totals["gpsimd"])

    @block.tensor
    def _(t):
        P.emit_stream("tensor", t, sems)

    @block.scalar
    def _(s):
        P.emit_stream("scalar", s, sems)

    @block.vector
    def _(v):
        P.emit_stream("vector", v, sems)

    ctx.close()
    return nc


def kernel(L0_attn_x, L0_attn_Wq, L0_attn_Wk, L0_attn_Wv, L0_attn_Wo,
           L0_ffn_Wr, L0_ffn_br, L0_ffn_Wg_s, L0_ffn_bg_s, L0_ffn_Wu_s,
           L0_ffn_bu_s, L0_ffn_Wd_s, L0_ffn_bd_s, L0_ffn_Wg, L0_ffn_bg,
           L0_ffn_Wu, L0_ffn_bu, L0_ffn_Wd, L0_ffn_bd):
    x = np.ascontiguousarray(np.asarray(L0_attn_x, dtype=np.float32))
    ident = np.eye(128, dtype=np.float32)

    def f(a, shape=None):
        a = np.ascontiguousarray(np.asarray(a, dtype=np.float32))
        return a.reshape(shape) if shape is not None else a

    common = {
        "wq": f(L0_attn_Wq), "wk": f(L0_attn_Wk), "wv": f(L0_attn_Wv),
        "wo": f(L0_attn_Wo), "wr": f(L0_ffn_Wr), "br": f(L0_ffn_br, (1, NEXP)),
        "wgs": f(L0_ffn_Wg_s), "bgs": f(L0_ffn_bg_s, (1, NI)),
        "wus": f(L0_ffn_Wu_s), "bus": f(L0_ffn_bu_s, (1, NI)),
        "wds": f(L0_ffn_Wd_s), "bds": f(L0_ffn_bd_s, (1, E)),
        "wg": f(L0_ffn_Wg), "bg": f(L0_ffn_bg), "wu": f(L0_ffn_Wu),
        "bu": f(L0_ffn_bu), "wd": f(L0_ffn_Wd), "bd": f(L0_ffn_bd),
        "ident": ident,
    }
    nc = build_program(NB)
    in_maps = []
    for c in range(8):
        m = dict(common)
        m["x_in"] = x[c * NB:(c + 1) * NB]
        in_maps.append(m)
    res = run_bass_kernel_spmd(nc, in_maps, core_ids=list(range(8)))
    return np.concatenate([r["y_out"] for r in res.results], axis=0)


# version 14
# speedup vs baseline: 1.0315x; 1.0315x over previous
import numpy as np
import ml_dtypes
import concourse.bass as bass
import concourse.mybir as mybir
from concourse.bass_utils import run_bass_kernel_spmd

F32 = mybir.dt.float32
BF16 = mybir.dt.bfloat16
AF = mybir.ActivationFunctionType
ALU = mybir.AluOpType
AX = mybir.AxisListType

NB = 4
S = 2048
E = 1024
NH = 16
HD = 64
NI = 512
NEXP = 32
NKT = E // 128
NTT = S // 128
NPAIR = 8
GROUP = 1024
NG = S // GROUP
GT = GROUP // 128

ENGINES = ("sync", "gpsimd", "tensor", "scalar", "vector")


class Plan:

    def __init__(self):
        self.ops = []
        self.streams = {e: [] for e in ENGINES}
        self.spaces = {}
        self.dma_count = {"sync": 0, "gpsimd": 0}

    def op(self, engine, fn, r=(), w=(), dma=False, tag=""):
        idx = len(self.ops)
        rec = {"eng": engine, "fn": fn, "deps": set(), "inc": False,
               "dma": dma, "dma_i": (self.dma_count[engine] + 1) if dma else None,
               "tag": tag}
        if dma:
            self.dma_count[engine] += 1
        self.ops.append(rec)
        self.streams[engine].append(idx)

        def dep(p_idx, war_reader=False):
            if p_idx == idx:
                return
            p = self.ops[p_idx]
            if p["eng"] == engine and not p["dma"] and not dma:
                if engine == "tensor":
                    return
                if war_reader:
                    return
            rec["deps"].add(p_idx)

        for (sp, lo, hi) in r:
            s = self.spaces.setdefault(sp, {"w": [], "r": []})
            for (l2, h2, widx) in s["w"]:
                if lo < h2 and l2 < hi:
                    dep(widx)
            s["r"].append((lo, hi, idx))
        for (sp, lo, hi) in w:
            s = self.spaces.setdefault(sp, {"w": [], "r": []})
            for (l2, h2, widx) in s["w"]:
                if lo < h2 and l2 < hi:
                    dep(widx)
            for (l2, h2, ridx) in s["r"]:
                if lo < h2 and l2 < hi:
                    dep(ridx, war_reader=True)
            s["w"] = [t for t in s["w"] if not (lo <= t[0] and t[1] <= hi)]
            s["r"] = [t for t in s["r"] if not (lo <= t[0] and t[1] <= hi)]
            s["w"].append((lo, hi, idx))
        return idx

    def finalize(self):
        for rec in self.ops:
            for d in rec["deps"]:
                self.ops[d]["inc"] = True
        inc_no = {}
        counters = {e: 0 for e in ENGINES}
        for e in ENGINES:
            for idx in self.streams[e]:
                rec = self.ops[idx]
                if rec["dma"]:
                    inc_no[idx] = 16 * rec["dma_i"]
                elif rec["inc"]:
                    counters[e] += 1
                    inc_no[idx] = counters[e]
        self.inc_no = inc_no
        self.totals = counters
        self.dma_totals = {e: 16 * self.dma_count[e] for e in ("sync", "gpsimd")}

    def emit_stream(self, engine, eng, sems):
        max_waited = {}
        for idx in self.streams[engine]:
            rec = self.ops[idx]
            need = {}
            for d in rec["deps"]:
                p = self.ops[d]
                psem = ("dma_" + p["eng"]) if p["dma"] else p["eng"]
                v = self.inc_no[d]
                if v > need.get(psem, -1):
                    need[psem] = v
            for psem in sorted(need):
                v = need[psem]
                if max_waited.get(psem, 0) >= v:
                    continue
                eng.wait_ge(sems[psem], v)
                max_waited[psem] = v
            inst = rec["fn"](eng)
            if rec["dma"]:
                inst.then_inc(sems["dma_" + engine], 16)
            elif rec["inc"]:
                inst.then_inc(sems[engine], 1)


def build_program(n_batch=NB, dbg=()):
    nc = bass.Bass("TRN2", target_bir_lowering=False)

    d_x = nc.dram_tensor("x_in", [n_batch, S, E], F32, kind="ExternalInput").ap()
    d_wq = nc.dram_tensor("wq", [E, E], F32, kind="ExternalInput").ap()
    d_wk = nc.dram_tensor("wk", [E, E], F32, kind="ExternalInput").ap()
    d_wv = nc.dram_tensor("wv", [E, E], F32, kind="ExternalInput").ap()
    d_wo = nc.dram_tensor("wo", [E, E], F32, kind="ExternalInput").ap()
    d_wr = nc.dram_tensor("wr", [E, NEXP], F32, kind="ExternalInput").ap()
    d_br = nc.dram_tensor("br", [1, NEXP], F32, kind="ExternalInput").ap()
    d_wgs = nc.dram_tensor("wgs", [E, NI], F32, kind="ExternalInput").ap()
    d_bgs = nc.dram_tensor("bgs", [1, NI], F32, kind="ExternalInput").ap()
    d_wus = nc.dram_tensor("wus", [E, NI], F32, kind="ExternalInput").ap()
    d_bus = nc.dram_tensor("bus", [1, NI], F32, kind="ExternalInput").ap()
    d_wds = nc.dram_tensor("wds", [NI, E], F32, kind="ExternalInput").ap()
    d_bds = nc.dram_tensor("bds", [1, E], F32, kind="ExternalInput").ap()
    d_wg = nc.dram_tensor("wg", [NEXP, E, NI], F32, kind="ExternalInput").ap()
    d_bg = nc.dram_tensor("bg", [NEXP, NI], F32, kind="ExternalInput").ap()
    d_wu = nc.dram_tensor("wu", [NEXP, E, NI], F32, kind="ExternalInput").ap()
    d_bu = nc.dram_tensor("bu", [NEXP, NI], F32, kind="ExternalInput").ap()
    d_wd = nc.dram_tensor("wd", [NEXP, NI, E], F32, kind="ExternalInput").ap()
    d_bd = nc.dram_tensor("bd", [NEXP, E], F32, kind="ExternalInput").ap()
    d_ident = nc.dram_tensor("ident", [128, 128], F32, kind="ExternalInput").ap()
    d_out = nc.dram_tensor("y_out", [n_batch, S, E], F32, kind="ExternalOutput").ap()

    d_swqk = nc.dram_tensor("scr_wqk", [2, E, E], BF16, kind="Internal").ap()
    d_swg = nc.dram_tensor("scr_wg", [NEXP, E, NI], BF16, kind="Internal").ap()
    d_swu = nc.dram_tensor("scr_wu", [NEXP, E, NI], BF16, kind="Internal").ap()
    d_swd = nc.dram_tensor("scr_wd", [NEXP, NI, E], BF16, kind="Internal").ap()
    d_swgs = nc.dram_tensor("scr_wgs", [E, NI], BF16, kind="Internal").ap()
    d_swus = nc.dram_tensor("scr_wus", [E, NI], BF16, kind="Internal").ap()
    d_swds = nc.dram_tensor("scr_wds", [NI, E], BF16, kind="Internal").ap()

    dbg_out = {}
    for name, shape, dt in dbg:
        dbg_out[name] = nc.dram_tensor("dbg_" + name, list(shape), dt,
                                       kind="ExternalOutput").ap()

    from contextlib import ExitStack
    ctx = ExitStack()
    e_ = ctx.enter_context

    ar1 = e_(nc.sbuf_tensor("ar1", [128, 33280], mybir.dt.uint8))
    ar2 = e_(nc.sbuf_tensor("ar2", [128, 58368], mybir.dt.uint8))
    ar3 = e_(nc.sbuf_tensor("ar3", [128, 18432], mybir.dt.uint8))
    ar4 = e_(nc.sbuf_tensor("ar4", [128, 32768], mybir.dt.uint8))
    ar5 = e_(nc.sbuf_tensor("ar5", [128, 32768], mybir.dt.uint8))

    a1b, a2b = ar1.ap().bitcast(BF16), ar2.ap().bitcast(BF16)
    a2f = ar2.ap().bitcast(F32)
    a3b, a3f = ar3.ap().bitcast(BF16), ar3.ap().bitcast(F32)
    a4b = ar4.ap().bitcast(BF16)
    a5b, a5f = ar5.ap().bitcast(BF16), ar5.ap().bitcast(F32)

    t_vbuf = a1b
    t_y1nT = a1b

    def R_vbuf(h, tt):
        c = h * 1040 + tt * 65
        return ("ar1", 2 * c, 2 * (c + 65))

    def R_y1nT(kt, t0, t1):
        return ("ar1", 2 * (kt * 2048 + t0), 2 * (kt * 2048 + t1))

    def t_wqk(par, wi):
        return a2b[:, (par * 2 + wi) * 1024:(par * 2 + wi + 1) * 1024]

    def R_wqk(par, wi):
        return ("ar2", (par * 2 + wi) * 2048, (par * 2 + wi + 1) * 2048)

    t_ATr = [a2b[:, 4096 + i * 1024: 4096 + (i + 1) * 1024] for i in range(2)]

    def R_ATr(i):
        return ("ar2", 8192 + i * 2048, 8192 + (i + 1) * 2048)

    t_ustg = a2f[:, 3072:3584]
    R_ustg = ("ar2", 12288, 14336)
    t_rbc = a2f[:, 3584:4096]
    R_rbc = ("ar2", 14336, 16384)

    def t_qhs(par, i):
        return a2b[:, 8192 + (par * 4 + i) * 2048: 8192 + (par * 4 + i + 1) * 2048]

    def R_qhs(par, i):
        return ("ar2", 16384 + (par * 4 + i) * 4096, 16384 + (par * 4 + i + 1) * 4096)

    t_wo2 = [a2b[:, 8192 + j * 4096: 8192 + (j + 1) * 4096] for j in range(2)]

    def R_wo2(j):
        return ("ar2", 16384 + j * 8192, 16384 + (j + 1) * 8192)

    t_wv1 = a2b[:, 24576:28672]
    R_wv1 = ("ar2", 49152, 57344)

    t_wge = [a2b[:, i * 4096: (i + 1) * 4096] for i in range(2)]
    t_wue = [a2b[:, 4096 * 2 + i * 4096: 4096 * 2 + (i + 1) * 4096] for i in range(2)]
    t_wde = [a2b[:, 4096 * 4 + i * 4096: 4096 * 4 + (i + 1) * 4096] for i in range(2)]
    t_silu = a2b[:, 24576:26624]
    t_usb = a2b[:, 26624:28672]

    def R_wge(i):
        return ("ar2", i * 8192, (i + 1) * 8192)

    def R_wue(i):
        return ("ar2", 16384 + i * 8192, 16384 + (i + 1) * 8192)

    def R_wde(i):
        return ("ar2", 32768 + i * 8192, 32768 + (i + 1) * 8192)

    R_silu = ("ar2", 49152, 53248)
    R_usb = ("ar2", 53248, 57344)

    t_xroll = [a3b[:, i * 1024: (i + 1) * 1024] for i in range(2)]

    def R_xroll(i):
        return ("ar3", i * 2048, (i + 1) * 2048)

    t_projr = [a3b[:, i * 512: (i + 1) * 512] for i in range(2)]

    def R_projr(i):
        return ("ar3", i * 1024, (i + 1) * 1024)

    t_y1sc = [a3f[:, 512 + i * 1024: 512 + (i + 1) * 1024] for i in range(4)]

    def R_y1sc(i, lo=0, hi=1024):
        return ("ar3", 2048 + i * 4096 + 4 * lo, 2048 + i * 4096 + 4 * hi)

    t_y1scB = [a2f[:, i * 1024: (i + 1) * 1024] for i in range(4)]

    def R_y1scB(i, lo=0, hi=1024):
        return ("ar2", i * 4096 + 4 * lo, i * 4096 + 4 * hi)

    t_act = [a3b[:, i * 4096: (i + 1) * 4096] for i in range(2)]

    def R_act(i):
        return ("ar3", i * 8192, (i + 1) * 8192)

    t_xroll1 = a3b[:, 8192:9216]
    R_xroll1 = ("ar3", 16384, 18432)

    t_xT = a4b

    def R_xT(kt, t0, t1):
        return ("ar4", 2 * (kt * 2048 + t0), 2 * (kt * 2048 + t1))

    t_aoT = a5b

    def R_aoT(hp, t0, t1):
        return ("ar5", 2 * (hp * 2048 + t0), 2 * (hp * 2048 + t1))

    t_racc = a5f

    def R_racc(tsub):
        return ("ar5", tsub * 4096, (tsub + 1) * 4096)

    def sb(name, p, cols, dt):
        return e_(nc.sbuf_tensor(name, [p, cols], dt)).ap()

    t_ident = sb("identt", 128, 128, BF16)
    t_ones_r = sb("ones_r", 1, 128, BF16)
    t_oblk = sb("oblk", 128, 33, BF16)
    t_br = sb("brr", 1, NEXP, BF16)
    t_bds = sb("bdsr", 1, E, BF16)
    t_bd = sb("bdm", NEXP, E, BF16)
    t_bgT = sb("bgT", 128, 128, F32)
    t_buT = sb("buT", 128, 128, F32)
    t_bgsT = sb("bgsT", 128, 4, F32)
    t_busT = sb("busT", 128, 4, F32)
    t_wr = sb("wrm", 128, NKT * NEXP, BF16)
    t_eps = sb("epsc", 128, 1, F32)
    t_onecol = sb("onecol", 128, 1, F32)
    t_xf32 = [sb(f"xf32_{i}", 128, 1024, F32) for i in range(2)]
    t_y1nb = sb("y1nb", 128, 1024, BF16)
    t_lnt = t_y1nb
    t_means = sb("meanst", 128, 16, F32)
    t_negm2 = [sb(f"negm{i}", 128, 1, F32) for i in range(2)]
    t_vsum2 = [sb(f"vsum{i}", 128, 1, F32) for i in range(2)]
    t_rstd2 = [sb(f"rstd{i}", 128, 1, F32) for i in range(2)]
    t_aff = sb("aff", 128, NEXP, F32)
    t_top8 = sb("top8", 128, 8, F32)
    t_zap = sb("zap", 128, NEXP, F32)
    t_wmask = sb("wmask", 128, NEXP, F32)
    t_wsum = sb("wsum", 128, 1, F32)
    t_wrec = sb("wrec", 128, 1, F32)
    t_ws = sb("ws", 128, NTT * NEXP, F32)
    t_wT = sb("wTm", NEXP, S, BF16)
    t_urec = sb("urec", 128, 1, F32)
    t_outs = [sb(f"outs_{i}", 128, 1024, F32) for i in range(2)]
    t_wstage = sb("wstage", 128, 2 * NEXP, BF16)
    t_mscr = sb("mscr", 128, 1024, BF16)
    t_sqqs = [sb(f"sqq{i}", 128, 512, BF16) for i in range(2)]
    t_rr = sb("rr", 128, 512, BF16)
    t_ones64 = sb("ones64", 128, 64, BF16)

    def R_outs(blo, bhi):
        return ("outs", blo, bhi)

    def y1nbQ(tl):
        return t_outs[tl // 2].bitcast(BF16)[:, (tl % 2) * 1024:(tl % 2) * 1024 + 1024]

    def R_y1nbQ(tl, lo=0, hi=1024):
        base = (tl // 2) * 4096 + (tl % 2) * 2048
        return ("outs", base + 2 * lo, base + 2 * hi)

    def R(name):
        return (name, 0, 1 << 30)

    def Rws(tt, e0, e1):
        return ("ws", 4 * (tt * NEXP + e0), 4 * (tt * NEXP + e1))

    def RwT(t0, t1):
        return ("wT", 2 * t0, 2 * t1)

    psA = e_(nc.psum_tensor("psA", [128, 512], F32)).ap()
    psB = e_(nc.psum_tensor("psB", [128, 512], F32)).ap()
    psC = e_(nc.psum_tensor("psC", [128, 512], F32)).ap()
    psD = e_(nc.psum_tensor("psD", [128, 512], F32)).ap()
    psE = e_(nc.psum_tensor("psE", [128, 1024], F32)).ap()
    psF = e_(nc.psum_tensor("psF", [128, 1024], F32)).ap()
    ps1 = [psA, psB, psC, psD,
           psE[:, 0:512], psE[:, 512:1024], psF[:, 0:512], psF[:, 512:1024]]
    ps1b = [p.bitcast(BF16) for p in ps1]
    ps1r = [R("psA"), R("psB"), R("psC"), R("psD"),
            ("psE", 0, 1), ("psE", 1, 2), ("psF", 0, 1), ("psF", 1, 2)]
    R_psE = ("psE", 0, 2)
    R_psF = ("psF", 0, 2)

    P = Plan()

    def dma_g(dst, src, r=(), w=(), tag=""):
        P.op("gpsimd", lambda g: g.dma_start(dst, src), r=r, w=w, dma=True, tag=tag)

    def dma_s(dst, src, r=(), w=(), tag=""):
        P.op("sync", lambda s: s.dma_start(dst, src), r=r, w=w, dma=True, tag=tag)

    dma_g(t_ident[:, :], d_ident, [], [R("ident")], "ident")
    P.op("vector", lambda v: v.memset(t_ones_r[:, :], 1.0), w=[R("ones_r")])
    P.op("vector", lambda v: v.memset(t_oblk[:, :], 0.0), w=[R("oblk")])
    P.op("vector", lambda v: v.memset(t_oblk[0:64, 0:1], 1.0),
         r=[R("oblk")], w=[("oblk", 0, 0)])
    P.op("vector", lambda v: v.memset(t_oblk[64:128, 32:33], 1.0),
         r=[R("oblk")], w=[("oblk", 0, 0)])
    P.op("vector", lambda v: v.memset(t_eps[:, :], 1e-5), w=[R("eps")])
    P.op("vector", lambda v: v.memset(t_onecol[:, :], 1.0), w=[R("onecol")])
    P.op("vector", lambda v: v.memset(t_ones64[:, :], 1.0), w=[R("ones64")])
    P.op("vector", lambda v: v.memset(
        t_vbuf[:, 0:NH * 1040].rearrange("p (n c) -> p n c", c=65)[:, :, 64:65], 1.0),
        w=[("ar1", 0, 2 * NH * 1040)], tag="vones")

    dma_g(t_br[:, :], d_br, [], [R("br")], "br")
    dma_g(t_bds[:, :], d_bds, [], [R("bds")], "bds")
    dma_g(t_bd[:, :], d_bd, [], [R("bd")], "bd")
    dma_g(t_wr[:, :].rearrange("p (k e) -> p k e", e=NEXP),
          d_wr.rearrange("(k p) e -> p k e", p=128), [], [R("wr")], "wr")

    t_bstage = t_wT[:, 0:512]
    for nm, src, dstT in (("bg", d_bg, t_bgT), ("bu", d_bu, t_buT)):
        dma_g(t_bstage, src, [], [RwT(0, 512)], nm)
        for mt in range(4):
            pi = mt
            P.op("tensor", lambda t, mt=mt, pi=pi: t.transpose(
                ps1b[pi][:, 0:32], t_bstage[:, mt * 128:(mt + 1) * 128],
                t_ident[0:32, 0:32]),
                r=[RwT(0, 512), R("ident")], w=[ps1r[pi]])
            P.op("scalar", lambda s, mt=mt, pi=pi, dstT=dstT: s.copy(
                dstT.rearrange("p (e m) -> p e m", m=4)[:, :, mt:mt + 1],
                ps1b[pi][:, 0:32].rearrange("p (e m) -> p e m", m=1)),
                r=[ps1r[pi]], w=[(id(dstT).__str__(), 0, 1)], tag=nm + "T")

    def R_bT(dstT):
        return (id(dstT).__str__(), 0, 1)

    for nm, src, dstT in (("bgs", d_bgs, t_bgsT), ("bus", d_bus, t_busT)):
        dma_g(t_bstage[0:1, :], src, [], [RwT(0, 512)], nm)
        for mt in range(4):
            pi = mt
            P.op("tensor", lambda t, mt=mt, pi=pi: t.transpose(
                ps1b[pi][:, 0:1], t_bstage[0:1, mt * 128:(mt + 1) * 128],
                t_ident[0:1, 0:1]),
                r=[RwT(0, 512), R("ident")], w=[ps1r[pi]])
            P.op("scalar", lambda s, mt=mt, pi=pi, dstT=dstT: s.copy(
                dstT[:, mt:mt + 1], ps1b[pi][:, 0:1]),
                r=[ps1r[pi]], w=[R_bT(dstT)], tag=nm + "T")

    for wi, dsrc in ((0, d_wq), (1, d_wk)):
        for kt in range(NKT):
            st = t_xf32[kt % 2]
            skey = ("xf32", kt % 2, kt % 2 + 1)
            dma_g(st[:, :], dsrc[kt * 128:(kt + 1) * 128, :], [], [skey], "wqkld")
            P.op("vector", lambda v, st=st: v.tensor_reduce(
                t_means[:, :].rearrange("p (g o) -> p g o", o=1),
                st[:, :].rearrange("p (g c) -> p g c", c=64), AX.X, ALU.add),
                r=[skey], w=[R("means")])
            P.op("vector", lambda v: v.tensor_scalar(
                t_means[:, :], t_means[:, :], 1.0 / 64.0, None, ALU.mult),
                r=[R("means")], w=[R("means")])
            for g in range(16):
                P.op("vector", lambda v, st=st, g=g: v.tensor_scalar(
                    t_lnt[:, g * 64:(g + 1) * 64], st[:, g * 64:(g + 1) * 64],
                    t_means[:, g:g + 1], None, ALU.subtract),
                    r=[skey, R("means")], w=[("y1nb", g, g + 1)])
            dma_s(d_swqk[wi, kt * 128:(kt + 1) * 128, :], t_lnt[:, :],
                  [("y1nb", 0, 16)], [("swqk", wi * 8 + kt, wi * 8 + kt + 1)], "wqkst")

    def emit_A_dma(bn, tt, xr, rx, queue):
        (dma_s if queue == "sync" else dma_g)(
            xr[:, :], d_x[bn, tt * 128:(tt + 1) * 128, :], [], [rx], "xld")

    def emit_A_tr(tt, xr, rx):
        for et in range(NKT):
            pi = (tt * NKT + et) % 4
            P.op("tensor", lambda t, xr=xr, et=et, pi=pi: t.transpose(
                ps1b[pi][:, 0:128], xr[:, et * 128:(et + 1) * 128], t_ident),
                r=[rx, R("ident")], w=[ps1r[pi]], tag="xT")
            P.op("scalar", lambda s, et=et, tt=tt, pi=pi: s.copy(
                t_xT[:, et * 2048 + tt * 128: et * 2048 + (tt + 1) * 128],
                ps1b[pi][:, 0:128]),
                r=[ps1r[pi]], w=[R_xT(et, tt * 128, (tt + 1) * 128)])

    def emit_A_tile(bn, tt, xr, rx):
        emit_A_dma(bn, tt, xr, rx, "gpsimd")
        emit_A_tr(tt, xr, rx)

    pending_finish = []
    for b in range(n_batch):
        if b == 0:
            for tt in range(NTT):
                emit_A_tile(0, tt, t_xroll[tt % 2], R_xroll(tt % 2))

        for par in range(2):
            for i in range(4):
                P.op("vector", lambda v, par=par, i=i: v.memset(
                    t_qhs(par, i)[64:128, :], 0.0),
                    w=[R_qhs(par, i)], tag="qhz")

        def c1_maker(p):
            par = p % 2
            pending = []
            pend_ev = []
            state = {"j": 0}

            def flush():
                while pend_ev:
                    pend_ev.pop(0)()
                while pending:
                    pending.pop(0)()

            def mid():
                if pend_ev:
                    pend_ev.pop(0)()

            def emit_next():
                if state["j"] < 8:
                    emit_slice(state["j"])
                    state["j"] += 1

            def emit_slice(j):
                flush()
                sc, wi = j >> 1, j & 1
                wt, rwt = t_wqk(par, wi), R_wqk(par, wi)
                if j <= 1:
                    dma_s(wt[:, :].rearrange("p (k m) -> p k m", m=128),
                          d_swqk[wi].rearrange("(k p) m -> p k m", p=128)
                          [:, :, p * 128:(p + 1) * 128],
                          [("swqk", wi * 8, wi * 8 + 8)], [rwt], "wqkpld")
                ps, rps = (psC, R("psC")) if wi == 0 else (psD, R("psD"))
                for kt in range(NKT):
                    P.op("tensor", lambda t, ps=ps, wt=wt, kt=kt, sc=sc: t.matmul(
                        ps[:, :], wt[:, kt * 128:(kt + 1) * 128],
                        t_xT[:, kt * 2048 + sc * 512: kt * 2048 + (sc + 1) * 512],
                        start=(kt == 0), stop=(kt == NKT - 1)),
                        r=[R_xT(kt, sc * 512, (sc + 1) * 512), rwt], w=[rps],
                        tag="qkmm")
                sqq, rsqq = t_sqqs[wi], R(f"sqq{wi}")
                qa, ra = t_qhs(par, 2 * wi + 0), R_qhs(par, 2 * wi + 0)
                qb, rb = t_qhs(par, 2 * wi + 1), R_qhs(par, 2 * wi + 1)

                def evac(ps=ps, rps=rps, sqq=sqq, rsqq=rsqq, qa=qa, ra=ra,
                         qb=qb, rb=rb, sc=sc):
                    P.op("scalar", lambda s: s.activation(
                        sqq[:, :], ps[:, :], AF.Square),
                        r=[rps], w=[rsqq], tag="sq")
                    P.op("scalar", lambda s: s.copy(
                        qa[0:64, sc * 512:(sc + 1) * 512], ps[0:64, :]),
                        r=[rps], w=[ra], tag="qev")
                    P.op("scalar", lambda s: s.copy(
                        qb[0:64, sc * 512:(sc + 1) * 512], ps[64:128, :]),
                        r=[rps], w=[rb], tag="qev")
                pend_ev.append(evac)

                def var(ps=ps, rps=rps, sqq=sqq, rsqq=rsqq, qa=qa, ra=ra,
                        qb=qb, rb=rb, sc=sc):
                    P.op("tensor", lambda t: t.matmul(
                        ps[0:33, :], t_oblk[:, :], sqq[:, :], start=True, stop=True),
                        r=[rsqq, R("oblk")], w=[rps], tag="varmm")
                    P.op("scalar", lambda s: s.activation(
                        qa[64:65, sc * 512:(sc + 1) * 512], ps[0:1, :], AF.Copy,
                        scale=1.0 / 63.0),
                        r=[rps], w=[ra], tag="varev")
                    P.op("scalar", lambda s: s.activation(
                        qb[64:65, sc * 512:(sc + 1) * 512], ps[32:33, :], AF.Copy,
                        scale=1.0 / 63.0),
                        r=[rps], w=[rb], tag="varev")
                pending.append(var)
            return emit_next, flush, mid

        c1 = [c1_maker(p) for p in range(NPAIR)]

        dma_g(t_wv1[:, :].rearrange("p (k i) -> p k i", i=512),
              d_wv.rearrange("(k p) m -> p k m", p=128)[:, :, 0:512],
              [], [R_wv1], "wvld")
        for j in range(8):
            c1[0][0]()
            if pending_finish:
                pending_finish.pop(0)()
        assert not pending_finish
        P.op("vector", lambda v: v.memset(
            t_vbuf[:, 0:NH * 1040].rearrange("p (n c) -> p n c", c=65)[:, :, 64:65], 1.0),
            w=[("ar1", 0, 2 * NH * 1040)], tag="vones")
        for ec in range(2):
            if ec == 1:
                dma_g(t_wv1[:, :].rearrange("p (k i) -> p k i", i=512),
                      d_wv.rearrange("(k p) m -> p k m", p=128)[:, :, 512:1024],
                      [], [R_wv1], "wvld")
                for _ in range(3):
                    c1[1][0]()
            for tt in range(NTT):
                ps, rps = ps1[tt % 2], ps1r[tt % 2]
                for kt in range(NKT):
                    P.op("tensor", lambda t, ps=ps, kt=kt, tt=tt: t.matmul(
                        ps[:, :], t_xT[:, kt * 2048 + tt * 128: kt * 2048 + (tt + 1) * 128],
                        t_wv1[:, kt * 512:(kt + 1) * 512],
                        start=(kt == 0), stop=(kt == NKT - 1)),
                        r=[R_xT(kt, tt * 128, (tt + 1) * 128), R_wv1], w=[rps], tag="vmm")
                P.op("vector", lambda v, ps=ps, ec=ec, tt=tt: v.tensor_copy(
                    t_vbuf.rearrange("p (j c) -> p j c", c=1040)
                    [:, ec * 8:(ec + 1) * 8, tt * 65: tt * 65 + 64],
                    ps[:, :].rearrange("p (j c) -> p j c", c=64)),
                    r=[rps], w=[R_vbuf(ec * 8 + j, tt) for j in range(8)],
                    tag="vev")

        def emit_umm(h, tp):
            at = t_ATr[tp % 2]
            for half in range(2):
                tt = 2 * tp + half
                P.op("tensor", lambda t, h=h, tt=tt, at=at, half=half: t.matmul(
                    psA[0:65, 0:512],
                    t_vbuf[:, h * 1040 + tt * 65: h * 1040 + (tt + 1) * 65],
                    at[:, half * 512:(half + 1) * 512],
                    start=(tt == 0), stop=(tt == NTT - 1)),
                    r=[R_vbuf(h, tt), R_ATr(tp % 2)], w=[ps1r[0]], tag="ummv")

        def emit_c2_step(hp, hl, sc, mid_cb=None):
            par = hp % 2
            h = hp * 2 + hl
            qh, rqh = t_qhs(par, hl), R_qhs(par, hl)
            kh, rkh = t_qhs(par, 2 + hl), R_qhs(par, 2 + hl)
            for tp in range(8):
                if tp == 5 and mid_cb:
                    mid_cb()
                wps = psE if tp % 2 == 0 else psF
                base = 4 if tp % 2 == 0 else 6
                for half in range(2):
                    tt = 2 * tp + half
                    P.op("tensor", lambda t, wps=wps, half=half, kh=kh, qh=qh,
                         tt=tt, sc=sc: t.matmul(
                             wps[:, half * 512:(half + 1) * 512],
                             kh[:, tt * 128:(tt + 1) * 128],
                             qh[:, sc * 512:(sc + 1) * 512], start=True, stop=True),
                        r=[rkh, rqh], w=[ps1r[base + half]], tag="smm")
                if tp >= 2:
                    emit_umm(h, tp - 2)
                at = t_ATr[tp % 2]
                P.op("scalar", lambda s, at=at, wps=wps: s.activation(
                    at[:, :], wps[:, 0:1024], AF.Exp, scale=1.0 / 64.0),
                    r=[ps1r[base], ps1r[base + 1]], w=[R_ATr(tp % 2)], tag="exp")
            emit_umm(h, 6)
            emit_umm(h, 7)
            if hl == 0:
                P.op("scalar", lambda s: s.copy(
                    t_ustg[0:64, :], psA[0:64, 0:512]),
                    r=[ps1r[0]], w=[R_ustg], tag="ucp")
            else:
                P.op("scalar", lambda s: s.copy(
                    t_ustg[64:128, :], psA[0:64, 0:512]),
                    r=[ps1r[0]], w=[R_ustg], tag="ucp")
            P.op("scalar", lambda s: s.activation(
                t_xf32[0][64:65, 0:512], psA[64:65, 0:512], AF.Ln),
                r=[ps1r[0]], w=[("xf32", 0, 1)], tag="rln")
            P.op("scalar", lambda s: s.activation(
                t_rr[64:65, 0:512], t_xf32[0][64:65, 0:512], AF.Exp, scale=-1.0),
                r=[("xf32", 0, 1)], w=[R("rr")], tag="rrec")

            def tail(hl=hl, hp=hp, sc=sc):
                P.op("tensor", lambda t: t.matmul(
                    psB[hl * 64:(hl + 1) * 64, 0:512], t_ones64[64:65, 0:64],
                    t_rr[64:65, 0:512], start=True, stop=True),
                    r=[R("ones64"), R("rr")], w=[R("psB")], tag="bcmm")
                P.op("scalar", lambda s: s.copy(
                    t_rbc[hl * 64:(hl + 1) * 64, :], psB[hl * 64:(hl + 1) * 64, :]),
                    r=[R("psB")], w=[R_rbc], tag="rbccp")
                P.op("vector", lambda v: v.tensor_tensor(
                    t_aoT[hl * 64:(hl + 1) * 64, hp * 2048 + sc * 512: hp * 2048 + (sc + 1) * 512],
                    t_ustg[hl * 64:(hl + 1) * 64, :], t_rbc[hl * 64:(hl + 1) * 64, :],
                    ALU.mult),
                    r=[R_ustg, R_rbc], w=[R_aoT(hp, sc * 512, (sc + 1) * 512)],
                    tag="aomul")
            return tail

        for hp in range(NPAIR):
            c1[hp][1]()
            for hl in range(2):
                for sc in range(4):
                    tail = emit_c2_step(
                        hp, hl, sc, c1[hp + 1][2] if hp + 1 < NPAIR else None)
                    if hp + 1 < NPAIR:
                        c1[hp + 1][0]()
                    tail()
            if hp == 6:
                for jw in range(2):
                    dma_g(t_wo2[jw][:, :].rearrange("p (k i) -> p k i", i=512),
                          d_wo.rearrange("(k p) m -> p k m", p=128)
                          [:, :, jw * 512:(jw + 1) * 512],
                          [], [R_wo2(jw)], "wold")


        def emit_router(tc):
            pi = (2 * (tc % 4)) % 4
            pr, rpr = ps1[pi], ps1r[pi]
            for kt in range(NKT):
                P.op("tensor", lambda t, kt=kt, tc=tc, pr=pr: t.matmul(
                    pr[:, 0:NEXP],
                    t_y1nT[:, kt * 2048 + tc * 128: kt * 2048 + (tc + 1) * 128],
                    t_wr[:, kt * NEXP:(kt + 1) * NEXP],
                    start=(kt == 0), stop=False),
                    r=[R_y1nT(kt, tc * 128, (tc + 1) * 128), R("wr")], w=[rpr],
                    tag="rmm")
            P.op("tensor", lambda t, pr=pr: t.matmul(
                pr[:, 0:NEXP], t_ones_r[:, :], t_br[:, :], start=False, stop=True),
                r=[R("ones_r"), R("br")], w=[rpr], tag="rbias")
            P.op("scalar", lambda s, pr=pr: s.activation(
                t_aff[:, :], pr[:, 0:NEXP], AF.Sigmoid),
                r=[rpr], w=[R("aff")], tag="sig")
            P.op("vector", lambda v: v.max(out=t_top8[:, :], in_=t_aff[:, :]),
                 r=[R("aff")], w=[R("top8")])
            P.op("vector", lambda v: v.match_replace(
                out=t_zap[:, :], in_to_replace=t_top8[:, :], in_values=t_aff[:, :],
                imm_value=0.0),
                r=[R("aff"), R("top8")], w=[R("zap"), R("top8")])
            P.op("vector", lambda v: v.tensor_sub(
                t_wmask[:, :], t_aff[:, :], t_zap[:, :]),
                r=[R("aff"), R("zap")], w=[R("wmask")])
            P.op("vector", lambda v: v.tensor_reduce(
                t_wsum[:, :], t_top8[:, :], AX.X, ALU.add),
                r=[R("top8")], w=[R("wsum")])
            P.op("vector", lambda v: v.tensor_scalar(
                t_wsum[:, :], t_wsum[:, :], 1e-9, None, ALU.add),
                r=[R("wsum")], w=[R("wsum")])
            P.op("vector", lambda v: v.reciprocal(t_wrec[:, :], t_wsum[:, :]),
                 r=[R("wsum")], w=[R("wrec")])
            P.op("vector", lambda v, tc=tc: v.tensor_scalar(
                t_ws[:, tc * NEXP:(tc + 1) * NEXP], t_wmask[:, :], t_wrec[:, :],
                None, ALU.mult),
                r=[R("wmask"), R("wrec")], w=[Rws(tc, 0, NEXP)], tag="wnorm")
            sl = tc % 2
            P.op("vector", lambda v, tc=tc, sl=sl: v.tensor_copy(
                t_wstage[:, sl * NEXP:(sl + 1) * NEXP],
                t_ws[:, tc * NEXP:(tc + 1) * NEXP]),
                r=[Rws(tc, 0, NEXP)], w=[("wstage", sl, sl + 1)], tag="wbf")

        def emit_wT(tc):
            sl = tc % 2
            pj = (2 * (tc % 4) + 1) % 4
            P.op("tensor", lambda t, sl=sl, pj=pj: t.transpose(
                ps1b[pj][0:NEXP, 0:128], t_wstage[:, sl * NEXP:(sl + 1) * NEXP],
                t_ident),
                r=[("wstage", sl, sl + 1), R("ident")], w=[ps1r[pj]], tag="wTmm")
            P.op("scalar", lambda s, tc=tc, pj=pj: s.copy(
                t_wT[:, tc * 128:(tc + 1) * 128], ps1b[pj][0:NEXP, 0:128]),
                r=[ps1r[pj]], w=[RwT(tc * 128, (tc + 1) * 128)], tag="wTev")

        def emit_deferred(tc):
            if tc >= 1:
                emit_wT(tc - 1)
            tl = tc % 4
            for et in range(NKT):
                pj = 4 + et % 4
                P.op("tensor", lambda t, et=et, tl=tl, pj=pj: t.transpose(
                    ps1b[pj][:, 0:128], y1nbQ(tl)[:, et * 128:(et + 1) * 128],
                    t_ident),
                    r=[R_y1nbQ(tl, et * 128, (et + 1) * 128), R("ident")],
                    w=[ps1r[pj]], tag="y1T")
                P.op("scalar", lambda s, et=et, tc=tc, pj=pj: s.copy(
                    t_y1nT[:, et * 2048 + tc * 128: et * 2048 + (tc + 1) * 128],
                    ps1b[pj][:, 0:128]),
                    r=[ps1r[pj]], w=[R_y1nT(et, tc * 128, (tc + 1) * 128)],
                    tag="y1Tev")
            emit_router(tc)
            if tc == NTT - 1:
                emit_wT(tc)

        dma_s(t_xf32[0][:, :], d_x[b, 0:128, :], [], [("xf32", 0, 1)], "xres")
        d4flush = [(lambda tl=tl: emit_deferred(12 + tl)) for tl in range(4)]
        for sc in range(4):
            for tl in range(4):
                if sc > 0:
                    emit_deferred((sc - 1) * 4 + tl)
                if sc < 4:
                    tt = sc * 4 + tl
                    if tt + 1 < NTT:
                        p2n = (tt + 1) % 2
                        dma_s(t_xf32[p2n][:, :], d_x[b, (tt + 1) * 128:(tt + 2) * 128, :],
                              [], [("xf32", p2n, p2n + 1)], "xres")
                    ysel, Rysel = ((t_y1sc, R_y1sc) if sc % 2 == 1
                                   else (t_y1scB, R_y1scB))
                    y1 = ysel[tl]
                    ry1 = Rysel(tl)
                    rx = ("xf32", tt % 2, tt % 2 + 1)
                    for half in range(2):
                        pi = (2 * tl + half) % 4
                        ps, rps = ps1[pi], ps1r[pi]
                        wvb, rwv = t_wo2[half], R_wo2(half)
                        for kt in range(NKT):
                            P.op("tensor", lambda t, ps=ps, kt=kt, tt=tt, wvb=wvb: t.matmul(
                                ps[:, :],
                                t_aoT[:, kt * 2048 + tt * 128: kt * 2048 + (tt + 1) * 128],
                                wvb[:, kt * 512:(kt + 1) * 512],
                                start=(kt == 0), stop=(kt == NKT - 1)),
                                r=[R_aoT(kt, tt * 128, (tt + 1) * 128), rwv],
                                w=[rps], tag="pmm")
                        P.op("vector", lambda v, ps=ps, y1=y1, half=half,
                             xr=t_xf32[tt % 2]: v.tensor_add(
                                 y1[:, half * 512:(half + 1) * 512], ps[:, :],
                                 xr[:, half * 512:(half + 1) * 512]),
                             r=[rps, rx], w=[Rysel(tl, half * 512, (half + 1) * 512)],
                             tag="resadd")
                    p2 = tl % 2
                    negm, vsum, rstd = t_negm2[p2], t_vsum2[p2], t_rstd2[p2]
                    rng, rvs, rrs = (f"negm{p2}", 0, 1), (f"vsum{p2}", 0, 1), (f"rstd{p2}", 0, 1)
                    P.op("vector", lambda v, y1=y1, negm=negm: v.tensor_scalar(
                        t_mscr[:, :], y1[:, :], -1.0 / 1024.0, None,
                        ALU.mult, ALU.add, accum_out=negm[:, :]),
                        r=[ry1], w=[R("mscr"), rng], tag="lnmean")
                    P.op("scalar", lambda s, y1=y1, negm=negm, vsum=vsum: s.activation(
                        t_y1nb[:, :], y1[:, :], AF.Square, bias=negm[:, :],
                        accum_out=vsum[:, :]),
                        r=[ry1, rng], w=[R("y1nb"), rvs], tag="lnsq")
                    P.op("scalar", lambda s, vsum=vsum, rstd=rstd: s.activation(
                        rstd[:, :], vsum[:, :], AF.Sqrt, bias=t_eps[:, :],
                        scale=1.0 / 1024.0),
                        r=[rvs, R("eps")], w=[rrs], tag="lnsqrt")
                    P.op("vector", lambda v, rstd=rstd: v.reciprocal(rstd[:, :], rstd[:, :]),
                         r=[rrs], w=[rrs])
                    P.op("vector", lambda v, y1=y1, negm=negm, rstd=rstd, tl=tl: v.tensor_scalar(
                        y1nbQ(tl)[:, :], y1[:, :], negm[:, :], rstd[:, :],
                        ALU.add, ALU.mult),
                        r=[ry1, rng, rrs], w=[R_y1nbQ(tl)], tag="ln1n")

        def a_hook(k):
            if k >= 1:
                emit_A_tr(k - 1, t_xroll1, R_xroll1)
            if k < NTT:
                emit_A_dma(b + 1, k, t_xroll1, R_xroll1, "gpsimd")
        a_next = ([(lambda k=k: a_hook(k)) for k in range(NTT + 1)]
                  if b + 1 < n_batch else [])
        for g in range(NG):
            first = (b == 0 and g == 0)

            def wsrc(ve):
                if ve == 0:
                    return (d_wgs, d_wus, d_wds, d_swgs, d_swus, d_swds, NEXP * 3)
                e = ve - 1
                return (d_wg[e], d_wu[e], d_wd[e], d_swg[e], d_swu[e], d_swd[e],
                        e * 3)

            def plan_expert_dma(ve):
                sl = ve % 2
                sg_, su_, sd_, cg_, cu_, cd_, sx = wsrc(ve)
                if first:
                    dma_g(t_wge[sl][:, :].rearrange("p (k i) -> p k i", i=512),
                          sg_.rearrange("(k p) i -> p k i", p=128),
                          [], [R_wge(sl)], "wgld")
                    dma_s(cg_.rearrange("(k p) i -> p k i", p=128),
                          t_wge[sl][:, :].rearrange("p (k i) -> p k i", i=512),
                          [R_wge(sl)], [("sexp", sx, sx + 1)], "wgst")
                    dma_g(t_wue[sl][:, :].rearrange("p (k i) -> p k i", i=512),
                          su_.rearrange("(k p) i -> p k i", p=128),
                          [], [R_wue(sl)], "wuld")
                    dma_s(cu_.rearrange("(k p) i -> p k i", p=128),
                          t_wue[sl][:, :].rearrange("p (k i) -> p k i", i=512),
                          [R_wue(sl)], [("sexp", sx + 1, sx + 2)], "wust")
                    dma_g(t_wde[sl][:, :].rearrange("p (k m) -> p k m", m=1024),
                          sd_.rearrange("(k p) m -> p k m", p=128),
                          [], [R_wde(sl)], "wdld")
                    dma_s(cd_.rearrange("(k p) m -> p k m", p=128),
                          t_wde[sl][:, :].rearrange("p (k m) -> p k m", m=1024),
                          [R_wde(sl)], [("sexp", sx + 2, sx + 3)], "wdst")
                else:
                    dma_g(t_wge[sl][:, :].rearrange("p (k i) -> p k i", i=512),
                          cg_.rearrange("(k p) i -> p k i", p=128),
                          [("sexp", sx, sx + 1)], [R_wge(sl)], "wgld")
                    dma_g(t_wue[sl][:, :].rearrange("p (k i) -> p k i", i=512),
                          cu_.rearrange("(k p) i -> p k i", p=128),
                          [("sexp", sx + 1, sx + 2)], [R_wue(sl)], "wuld")
                    dma_g(t_wde[sl][:, :].rearrange("p (k m) -> p k m", m=1024),
                          cd_.rearrange("(k p) m -> p k m", p=128),
                          [("sexp", sx + 2, sx + 3)], [R_wde(sl)], "wdld")

            def emit_gu(ve, filler=None):
                sl = ve % 2
                wgt, rwg = t_wge[sl], R_wge(sl)
                wut, rwu = t_wue[sl], R_wue(sl)
                if ve == 0:
                    bgc, buc = t_bgsT, t_busT
                    bge = bue = 0
                else:
                    bgc, buc = t_bgT, t_buT
                    bge = bue = (ve - 1) * 4
                act = t_act[ve % 2]
                ract = R_act(ve % 2)
                for sub in range(2):
                    t0 = g * GROUP + sub * 512
                    for mt in range(4):
                        ps, rps = ps1[mt % 2], ps1r[mt % 2]
                        ps2, rps2 = ps1[2 + mt % 2], ps1r[2 + mt % 2]
                        for kt in range(NKT):
                            P.op("tensor", lambda t, ps=ps, wgt=wgt, kt=kt, mt=mt, t0=t0: t.matmul(
                                ps[:, :], wgt[:, kt * 512 + mt * 128: kt * 512 + (mt + 1) * 128],
                                t_y1nT[:, kt * 2048 + t0: kt * 2048 + t0 + 512],
                                start=(kt == 0), stop=(kt == NKT - 1)),
                                r=[R_y1nT(kt, t0, t0 + 512), rwg], w=[rps], tag="gmm")
                        for kt in range(NKT):
                            P.op("tensor", lambda t, ps2=ps2, wut=wut, kt=kt, mt=mt, t0=t0: t.matmul(
                                ps2[:, :], wut[:, kt * 512 + mt * 128: kt * 512 + (mt + 1) * 128],
                                t_y1nT[:, kt * 2048 + t0: kt * 2048 + t0 + 512],
                                start=(kt == 0), stop=(kt == NKT - 1)),
                                r=[R_y1nT(kt, t0, t0 + 512), rwu], w=[rps2], tag="umm2")
                        P.op("scalar", lambda s, ps=ps, mt=mt, bgc=bgc, bge=bge: s.activation(
                            t_silu[:, mt * 512:(mt + 1) * 512], ps[:, :], AF.Silu,
                            bias=bgc[:, bge + mt: bge + mt + 1]),
                            r=[rps, R_bT(bgc)], w=[("ar2", 49152 + mt * 1024, 49152 + (mt + 1) * 1024)],
                            tag="silu")
                        P.op("vector", lambda v, ps2=ps2, act=act, sub=sub, mt=mt,
                             buc=buc, bue=bue: v.scalar_tensor_tensor(
                                 act[:, sub * 2048 + mt * 512: sub * 2048 + (mt + 1) * 512],
                                 ps2[:, :], buc[:, bue + mt: bue + mt + 1],
                                 t_silu[:, mt * 512:(mt + 1) * 512],
                                 ALU.add, ALU.mult),
                             r=[rps2, R_bT(buc),
                                ("ar2", 49152 + mt * 1024, 49152 + (mt + 1) * 1024)],
                             w=[("ar3", (ve % 2) * 8192 + sub * 4096 + mt * 1024,
                                (ve % 2) * 8192 + sub * 4096 + (mt + 1) * 1024)],
                             tag="actm")
                        if filler:
                            filler.pop(0)()

            def emit_d(vd):
                wdt, rwd = t_wde[vd % 2], R_wde(vd % 2)
                act = t_act[vd % 2]
                ract = R_act(vd % 2)
                for tsub in range(GT):
                    sub, tl0 = tsub // 4, (tsub % 4) * 128
                    psd = psE if tsub % 2 == 0 else psF
                    rpsd = R_psE if tsub % 2 == 0 else R_psF
                    for it in range(4):
                        for hc in range(2):
                            P.op("tensor", lambda t, psd=psd, act=act, wdt=wdt,
                                 it=it, hc=hc, sub=sub, tl0=tl0: t.matmul(
                                     psd[:, hc * 512:(hc + 1) * 512],
                                     act[:, sub * 2048 + it * 512 + tl0: sub * 2048 + it * 512 + tl0 + 128],
                                     wdt[:, it * 1024 + hc * 512: it * 1024 + (hc + 1) * 512],
                                     start=(it == 0), stop=(it == 3)),
                                r=[ract, rwd], w=[rpsd], tag="dmm")
                    tt_g = g * GT + tsub
                    if vd == 0:
                        P.op("scalar", lambda s, psd=psd, tsub=tsub: s.copy(
                            t_racc[:, tsub * 1024:(tsub + 1) * 1024], psd[:, :]),
                            r=[rpsd], w=[R_racc(tsub)], tag="dcp")
                    else:
                        e = vd - 1
                        P.op("vector", lambda v, psd=psd, tsub=tsub, tt_g=tt_g, e=e:
                             v.scalar_tensor_tensor(
                                 t_racc[:, tsub * 1024:(tsub + 1) * 1024], psd[:, :],
                                 t_ws[:, tt_g * NEXP + e: tt_g * NEXP + e + 1],
                                 t_racc[:, tsub * 1024:(tsub + 1) * 1024],
                                 ALU.mult, ALU.add),
                             r=[rpsd, Rws(tt_g, e, e + 1), R_racc(tsub)],
                             w=[R_racc(tsub)], tag="dstt")

            def mk_finish(g, tsub, b=b):
                def go():
                    tt_g = g * GT + tsub
                    psd = psE if tsub % 2 == 0 else psF
                    rpsd = R_psE if tsub % 2 == 0 else R_psF
                    for hc in range(2):
                        P.op("tensor", lambda t, psd=psd, hc=hc, tt_g=tt_g: t.matmul(
                            psd[:, hc * 512:(hc + 1) * 512],
                            t_wT[:, tt_g * 128:(tt_g + 1) * 128],
                            t_bd[:, hc * 512:(hc + 1) * 512], start=True, stop=False),
                            r=[RwT(tt_g * 128, (tt_g + 1) * 128), R("bd")], w=[rpsd],
                            tag="bmm")
                        P.op("tensor", lambda t, psd=psd, hc=hc: t.matmul(
                            psd[:, hc * 512:(hc + 1) * 512], t_ones_r[:, :],
                            t_bds[:, hc * 512:(hc + 1) * 512], start=False, stop=False),
                            r=[R("ones_r"), R("bds")], w=[rpsd], tag="bmm2")
                        for et in range(4):
                            ec4 = hc * 4 + et
                            P.op("tensor", lambda t, psd=psd, hc=hc, et=et, ec4=ec4,
                                 tt_g=tt_g: t.matmul(
                                     psd[:, hc * 512 + et * 128: hc * 512 + (et + 1) * 128],
                                     t_y1nT[:, ec4 * 2048 + tt_g * 128: ec4 * 2048 + (tt_g + 1) * 128],
                                     t_ident, start=False, stop=(et == 3)),
                                r=[R_y1nT(ec4, tt_g * 128, (tt_g + 1) * 128), R("ident")],
                                w=[rpsd], tag="ymm")
                    P.op("vector", lambda v, psd=psd, tsub=tsub: v.tensor_add(
                        t_racc[:, tsub * 1024:(tsub + 1) * 1024], psd[:, :],
                        t_racc[:, tsub * 1024:(tsub + 1) * 1024]),
                        r=[rpsd, R_racc(tsub)], w=[R_racc(tsub)], tag="badd")
                    p2 = tsub % 2
                    oy = t_outs[p2]
                    roy = R_outs(p2 * 4096, (p2 + 1) * 4096)
                    negm, vsum, rstd = t_negm2[p2], t_vsum2[p2], t_rstd2[p2]
                    rng, rvs, rrs = (f"negm{p2}", 0, 1), (f"vsum{p2}", 0, 1), (f"rstd{p2}", 0, 1)
                    P.op("vector", lambda v, tsub=tsub, negm=negm: v.tensor_scalar(
                        t_mscr[:, :], t_racc[:, tsub * 1024:(tsub + 1) * 1024],
                        -1.0 / 1024.0, None, ALU.mult, ALU.add, accum_out=negm[:, :]),
                        r=[R_racc(tsub)], w=[R("mscr"), rng], tag="ln2mean")
                    P.op("scalar", lambda s, tsub=tsub, negm=negm, vsum=vsum: s.activation(
                        t_y1nb[:, :], t_racc[:, tsub * 1024:(tsub + 1) * 1024],
                        AF.Square, bias=negm[:, :], accum_out=vsum[:, :]),
                        r=[R_racc(tsub), rng], w=[R("y1nb"), rvs], tag="ln2sq")
                    P.op("scalar", lambda s, vsum=vsum, rstd=rstd: s.activation(
                        rstd[:, :], vsum[:, :], AF.Sqrt, bias=t_eps[:, :],
                        scale=1.0 / 1024.0),
                        r=[rvs, R("eps")], w=[rrs], tag="ln2sqrt")
                    P.op("vector", lambda v, rstd=rstd: v.reciprocal(rstd[:, :], rstd[:, :]),
                         r=[rrs], w=[rrs])
                    P.op("vector", lambda v, oy=oy, tsub=tsub, negm=negm, rstd=rstd: v.tensor_scalar(
                        oy[:, :], t_racc[:, tsub * 1024:(tsub + 1) * 1024],
                        negm[:, :], rstd[:, :], ALU.add, ALU.mult),
                        r=[R_racc(tsub), rng, rrs], w=[roy], tag="ln2n")
                    dma_s(d_out[b, tt_g * 128:(tt_g + 1) * 128, :], oy[:, :],
                          [roy], [], "outst")
                return go

            plan_expert_dma(0)
            plan_expert_dma(1)
            filler = d4flush if g == 0 else pending_finish
            for ve in range(NEXP + 2):
                if ve <= NEXP:
                    emit_gu(ve, filler if ve == 0 else None)
                if ve >= 1:
                    emit_d(ve - 1)
                if 1 <= ve < NEXP:
                    plan_expert_dma(ve + 1)
                if g == 1 and a_next and ve % 2 == 0:
                    a_next.pop(0)()
            pending_finish = [mk_finish(g, tsub) for tsub in range(GT)]

        if b == n_batch - 1:
            for go in pending_finish:
                go()
            pending_finish = []

        if b == 0:
            if "xT" in dbg_out:
                dma_s(dbg_out["xT"], t_xT[:, :], [R_xT(kt, 0, S) for kt in range(NKT)], [])
            if "aoT" in dbg_out:
                dma_s(dbg_out["aoT"], t_aoT[:, :], [R_aoT(hp, 0, S) for hp in range(NPAIR)], [])
            if "y1nT" in dbg_out:
                dma_s(dbg_out["y1nT"], t_y1nT[:, 0:NKT * S],
                      [R_y1nT(kt, 0, S) for kt in range(NKT)], [])
            if "ws" in dbg_out:
                dma_s(dbg_out["ws"], t_ws[:, :], [Rws(tt, 0, NEXP) for tt in range(NTT)], [])

    P.finalize()
    sems = {}
    block = None
    sem_objs = []
    for nm in ("tensor", "scalar", "vector", "gpsimd", "sync", "dma_gpsimd", "dma_sync"):
        h = ctx.enter_context(nc.semaphore("sem_" + nm))
        sems[nm] = h
    block = ctx.enter_context(nc.Block())

    @block.sync
    def _(sy):
        P.emit_stream("sync", sy, sems)
        if P.dma_totals["sync"]:
            sy.wait_ge(sems["dma_sync"], P.dma_totals["sync"])

    @block.gpsimd
    def _(gp):
        P.emit_stream("gpsimd", gp, sems)
        if P.dma_totals["gpsimd"]:
            gp.wait_ge(sems["dma_gpsimd"], P.dma_totals["gpsimd"])

    @block.tensor
    def _(t):
        P.emit_stream("tensor", t, sems)

    @block.scalar
    def _(s):
        P.emit_stream("scalar", s, sems)

    @block.vector
    def _(v):
        P.emit_stream("vector", v, sems)

    ctx.close()
    return nc


def kernel(L0_attn_x, L0_attn_Wq, L0_attn_Wk, L0_attn_Wv, L0_attn_Wo,
           L0_ffn_Wr, L0_ffn_br, L0_ffn_Wg_s, L0_ffn_bg_s, L0_ffn_Wu_s,
           L0_ffn_bu_s, L0_ffn_Wd_s, L0_ffn_bd_s, L0_ffn_Wg, L0_ffn_bg,
           L0_ffn_Wu, L0_ffn_bu, L0_ffn_Wd, L0_ffn_bd):
    x = np.ascontiguousarray(np.asarray(L0_attn_x, dtype=np.float32))
    ident = np.eye(128, dtype=np.float32)

    def f(a, shape=None):
        a = np.ascontiguousarray(np.asarray(a, dtype=np.float32))
        return a.reshape(shape) if shape is not None else a

    common = {
        "wq": f(L0_attn_Wq), "wk": f(L0_attn_Wk), "wv": f(L0_attn_Wv),
        "wo": f(L0_attn_Wo), "wr": f(L0_ffn_Wr), "br": f(L0_ffn_br, (1, NEXP)),
        "wgs": f(L0_ffn_Wg_s), "bgs": f(L0_ffn_bg_s, (1, NI)),
        "wus": f(L0_ffn_Wu_s), "bus": f(L0_ffn_bu_s, (1, NI)),
        "wds": f(L0_ffn_Wd_s), "bds": f(L0_ffn_bd_s, (1, E)),
        "wg": f(L0_ffn_Wg), "bg": f(L0_ffn_bg), "wu": f(L0_ffn_Wu),
        "bu": f(L0_ffn_bu), "wd": f(L0_ffn_Wd), "bd": f(L0_ffn_bd),
        "ident": ident,
    }
    nc = build_program(NB)
    in_maps = []
    for c in range(8):
        m = dict(common)
        m["x_in"] = x[c * NB:(c + 1) * NB]
        in_maps.append(m)
    res = run_bass_kernel_spmd(nc, in_maps, core_ids=list(range(8)))
    return np.concatenate([r["y_out"] for r in res.results], axis=0)


# version 15
# speedup vs baseline: 1.1252x; 1.0908x over previous
import numpy as np
import ml_dtypes
import concourse.bass as bass
import concourse.mybir as mybir
from concourse.bass_utils import run_bass_kernel_spmd

F32 = mybir.dt.float32
BF16 = mybir.dt.bfloat16
AF = mybir.ActivationFunctionType
ALU = mybir.AluOpType
AX = mybir.AxisListType

NB = 4
S = 2048
E = 1024
NH = 16
HD = 64
NI = 512
NEXP = 32
NKT = E // 128
NTT = S // 128
NPAIR = 8
GROUP = 1024
NG = S // GROUP
GT = GROUP // 128

ENGINES = ("sync", "gpsimd", "tensor", "scalar", "vector")


class Plan:

    def __init__(self):
        self.ops = []
        self.streams = {e: [] for e in ENGINES}
        self.spaces = {}
        self.dma_count = {"sync": 0, "gpsimd": 0}

    def op(self, engine, fn, r=(), w=(), dma=False, tag=""):
        idx = len(self.ops)
        rec = {"eng": engine, "fn": fn, "deps": set(), "inc": False,
               "dma": dma, "dma_i": (self.dma_count[engine] + 1) if dma else None,
               "tag": tag}
        if dma:
            self.dma_count[engine] += 1
        self.ops.append(rec)
        self.streams[engine].append(idx)

        def dep(p_idx, war_reader=False):
            if p_idx == idx:
                return
            p = self.ops[p_idx]
            if p["eng"] == engine and not p["dma"] and not dma:
                if engine == "tensor":
                    return
                if war_reader:
                    return
            rec["deps"].add(p_idx)

        for (sp, lo, hi) in r:
            s = self.spaces.setdefault(sp, {"w": [], "r": []})
            for (l2, h2, widx) in s["w"]:
                if lo < h2 and l2 < hi:
                    dep(widx)
            s["r"].append((lo, hi, idx))
        for (sp, lo, hi) in w:
            s = self.spaces.setdefault(sp, {"w": [], "r": []})
            for (l2, h2, widx) in s["w"]:
                if lo < h2 and l2 < hi:
                    dep(widx)
            for (l2, h2, ridx) in s["r"]:
                if lo < h2 and l2 < hi:
                    dep(ridx, war_reader=True)
            s["w"] = [t for t in s["w"] if not (lo <= t[0] and t[1] <= hi)]
            s["r"] = [t for t in s["r"] if not (lo <= t[0] and t[1] <= hi)]
            s["w"].append((lo, hi, idx))
        return idx

    def finalize(self):
        for rec in self.ops:
            for d in rec["deps"]:
                self.ops[d]["inc"] = True
        inc_no = {}
        counters = {e: 0 for e in ENGINES}
        for e in ENGINES:
            for idx in self.streams[e]:
                rec = self.ops[idx]
                if rec["dma"]:
                    inc_no[idx] = 16 * rec["dma_i"]
                elif rec["inc"]:
                    counters[e] += 1
                    inc_no[idx] = counters[e]
        self.inc_no = inc_no
        self.totals = counters
        self.dma_totals = {e: 16 * self.dma_count[e] for e in ("sync", "gpsimd")}

    def emit_stream(self, engine, eng, sems):
        max_waited = {}
        for idx in self.streams[engine]:
            rec = self.ops[idx]
            need = {}
            for d in rec["deps"]:
                p = self.ops[d]
                psem = ("dma_" + p["eng"]) if p["dma"] else p["eng"]
                v = self.inc_no[d]
                if v > need.get(psem, -1):
                    need[psem] = v
            for psem in sorted(need):
                v = need[psem]
                if max_waited.get(psem, 0) >= v:
                    continue
                eng.wait_ge(sems[psem], v)
                max_waited[psem] = v
            inst = rec["fn"](eng)
            if rec["dma"]:
                inst.then_inc(sems["dma_" + engine], 16)
            elif rec["inc"]:
                inst.then_inc(sems[engine], 1)


def build_program(n_batch=NB, dbg=()):
    nc = bass.Bass("TRN2", target_bir_lowering=False)

    d_x = nc.dram_tensor("x_in", [n_batch, S, E], F32, kind="ExternalInput").ap()
    d_wq = nc.dram_tensor("wq", [E, E], F32, kind="ExternalInput").ap()
    d_wk = nc.dram_tensor("wk", [E, E], F32, kind="ExternalInput").ap()
    d_wv = nc.dram_tensor("wv", [E, E], F32, kind="ExternalInput").ap()
    d_wo = nc.dram_tensor("wo", [E, E], F32, kind="ExternalInput").ap()
    d_wr = nc.dram_tensor("wr", [E, NEXP], F32, kind="ExternalInput").ap()
    d_br = nc.dram_tensor("br", [1, NEXP], F32, kind="ExternalInput").ap()
    d_wgs = nc.dram_tensor("wgs", [E, NI], F32, kind="ExternalInput").ap()
    d_bgs = nc.dram_tensor("bgs", [1, NI], F32, kind="ExternalInput").ap()
    d_wus = nc.dram_tensor("wus", [E, NI], F32, kind="ExternalInput").ap()
    d_bus = nc.dram_tensor("bus", [1, NI], F32, kind="ExternalInput").ap()
    d_wds = nc.dram_tensor("wds", [NI, E], F32, kind="ExternalInput").ap()
    d_bds = nc.dram_tensor("bds", [1, E], F32, kind="ExternalInput").ap()
    d_wg = nc.dram_tensor("wg", [NEXP, E, NI], F32, kind="ExternalInput").ap()
    d_bg = nc.dram_tensor("bg", [NEXP, NI], F32, kind="ExternalInput").ap()
    d_wu = nc.dram_tensor("wu", [NEXP, E, NI], F32, kind="ExternalInput").ap()
    d_bu = nc.dram_tensor("bu", [NEXP, NI], F32, kind="ExternalInput").ap()
    d_wd = nc.dram_tensor("wd", [NEXP, NI, E], F32, kind="ExternalInput").ap()
    d_bd = nc.dram_tensor("bd", [NEXP, E], F32, kind="ExternalInput").ap()
    d_ident = nc.dram_tensor("ident", [128, 128], F32, kind="ExternalInput").ap()
    d_out = nc.dram_tensor("y_out", [n_batch, S, E], F32, kind="ExternalOutput").ap()

    d_swqk = nc.dram_tensor("scr_wqk", [2, E, E], BF16, kind="Internal").ap()
    d_swg = nc.dram_tensor("scr_wg", [NEXP, E, NI], BF16, kind="Internal").ap()
    d_swu = nc.dram_tensor("scr_wu", [NEXP, E, NI], BF16, kind="Internal").ap()
    d_swd = nc.dram_tensor("scr_wd", [NEXP, NI, E], BF16, kind="Internal").ap()
    d_swgs = nc.dram_tensor("scr_wgs", [E, NI], BF16, kind="Internal").ap()
    d_swus = nc.dram_tensor("scr_wus", [E, NI], BF16, kind="Internal").ap()
    d_swds = nc.dram_tensor("scr_wds", [NI, E], BF16, kind="Internal").ap()

    dbg_out = {}
    for name, shape, dt in dbg:
        dbg_out[name] = nc.dram_tensor("dbg_" + name, list(shape), dt,
                                       kind="ExternalOutput").ap()

    from contextlib import ExitStack
    ctx = ExitStack()
    e_ = ctx.enter_context

    ar1 = e_(nc.sbuf_tensor("ar1", [128, 33280], mybir.dt.uint8))
    ar2 = e_(nc.sbuf_tensor("ar2", [128, 58368], mybir.dt.uint8))
    ar3 = e_(nc.sbuf_tensor("ar3", [128, 18432], mybir.dt.uint8))
    ar4 = e_(nc.sbuf_tensor("ar4", [128, 32768], mybir.dt.uint8))
    ar5 = e_(nc.sbuf_tensor("ar5", [128, 32768], mybir.dt.uint8))

    a1b, a2b = ar1.ap().bitcast(BF16), ar2.ap().bitcast(BF16)
    a2f = ar2.ap().bitcast(F32)
    a3b, a3f = ar3.ap().bitcast(BF16), ar3.ap().bitcast(F32)
    a4b = ar4.ap().bitcast(BF16)
    a5b, a5f = ar5.ap().bitcast(BF16), ar5.ap().bitcast(F32)

    t_vbuf = a1b
    t_y1nT = a1b

    def R_vbuf(h, tt):
        c = h * 1040 + tt * 65
        return ("ar1", 2 * c, 2 * (c + 65))

    def R_y1nT(kt, t0, t1):
        return ("ar1", 2 * (kt * 2048 + t0), 2 * (kt * 2048 + t1))

    def t_wqk(par, wi):
        return a2b[:, (par * 2 + wi) * 1024:(par * 2 + wi + 1) * 1024]

    def R_wqk(par, wi):
        return ("ar2", (par * 2 + wi) * 2048, (par * 2 + wi + 1) * 2048)

    t_ATr = [a2b[:, 4096 + i * 1024: 4096 + (i + 1) * 1024] for i in range(2)]

    def R_ATr(i):
        return ("ar2", 8192 + i * 2048, 8192 + (i + 1) * 2048)

    t_ustg = a2f[:, 3072:3584]
    R_ustg = ("ar2", 12288, 14336)
    t_rbc = a2f[:, 3584:4096]
    R_rbc = ("ar2", 14336, 16384)

    def t_qhs(par, i):
        return a2b[:, 8192 + (par * 4 + i) * 2048: 8192 + (par * 4 + i + 1) * 2048]

    def R_qhs(par, i):
        return ("ar2", 16384 + (par * 4 + i) * 4096, 16384 + (par * 4 + i + 1) * 4096)

    t_wo2 = [a2b[:, 8192 + j * 4096: 8192 + (j + 1) * 4096] for j in range(2)]

    def R_wo2(j):
        return ("ar2", 16384 + j * 8192, 16384 + (j + 1) * 8192)

    t_wv1 = a2b[:, 24576:28672]
    R_wv1 = ("ar2", 49152, 57344)

    t_wge = [a2b[:, i * 4096: (i + 1) * 4096] for i in range(2)]
    t_wue = [a2b[:, 4096 * 2 + i * 4096: 4096 * 2 + (i + 1) * 4096] for i in range(2)]
    t_wde = [a2b[:, 4096 * 4 + i * 4096: 4096 * 4 + (i + 1) * 4096] for i in range(2)]
    t_silu = a2b[:, 24576:26624]
    t_usb = a2b[:, 26624:28672]

    def R_wge(i):
        return ("ar2", i * 8192, (i + 1) * 8192)

    def R_wue(i):
        return ("ar2", 16384 + i * 8192, 16384 + (i + 1) * 8192)

    def R_wde(i):
        return ("ar2", 32768 + i * 8192, 32768 + (i + 1) * 8192)

    R_silu = ("ar2", 49152, 53248)
    R_usb = ("ar2", 53248, 57344)

    t_xroll = [a3b[:, i * 1024: (i + 1) * 1024] for i in range(2)]

    def R_xroll(i):
        return ("ar3", i * 2048, (i + 1) * 2048)

    t_projr = [a3b[:, i * 512: (i + 1) * 512] for i in range(2)]

    def R_projr(i):
        return ("ar3", i * 1024, (i + 1) * 1024)

    t_y1sc = [a3f[:, 512 + i * 1024: 512 + (i + 1) * 1024] for i in range(4)]

    def R_y1sc(i, lo=0, hi=1024):
        return ("ar3", 2048 + i * 4096 + 4 * lo, 2048 + i * 4096 + 4 * hi)

    t_y1scB = [a2f[:, i * 1024: (i + 1) * 1024] for i in range(4)]

    def R_y1scB(i, lo=0, hi=1024):
        return ("ar2", i * 4096 + 4 * lo, i * 4096 + 4 * hi)

    t_act = [a3b[:, i * 4096: (i + 1) * 4096] for i in range(2)]

    def R_act(i):
        return ("ar3", i * 8192, (i + 1) * 8192)

    t_xroll1 = a3b[:, 8192:9216]
    R_xroll1 = ("ar3", 16384, 18432)

    t_xT = a4b

    def R_xT(kt, t0, t1):
        return ("ar4", 2 * (kt * 2048 + t0), 2 * (kt * 2048 + t1))

    t_aoT = a5b

    def R_aoT(hp, t0, t1):
        return ("ar5", 2 * (hp * 2048 + t0), 2 * (hp * 2048 + t1))

    t_racc = a5f

    def R_racc(tsub):
        return ("ar5", tsub * 4096, (tsub + 1) * 4096)

    def sb(name, p, cols, dt):
        return e_(nc.sbuf_tensor(name, [p, cols], dt)).ap()

    t_ident = sb("identt", 128, 128, BF16)
    t_ones_r = sb("ones_r", 1, 128, BF16)
    t_oblk = sb("oblk", 128, 33, BF16)
    t_br = sb("brr", 1, NEXP, BF16)
    t_bds = sb("bdsr", 1, E, BF16)
    t_bd = sb("bdm", NEXP, E, BF16)
    t_bgT = sb("bgT", 128, 128, F32)
    t_buT = sb("buT", 128, 128, F32)
    t_bgsT = sb("bgsT", 128, 4, F32)
    t_busT = sb("busT", 128, 4, F32)
    t_wr = sb("wrm", 128, NKT * NEXP, BF16)
    t_eps = sb("epsc", 128, 1, F32)
    t_onecol = sb("onecol", 128, 1, F32)
    t_xf32 = [sb(f"xf32_{i}", 128, 1024, F32) for i in range(2)]
    t_y1nb = sb("y1nb", 128, 1024, BF16)
    t_lnt = t_y1nb
    t_means = sb("meanst", 128, 16, F32)
    t_negm2 = [sb(f"negm{i}", 128, 1, F32) for i in range(2)]
    t_vsum2 = [sb(f"vsum{i}", 128, 1, F32) for i in range(2)]
    t_rstd2 = [sb(f"rstd{i}", 128, 1, F32) for i in range(2)]
    t_aff = sb("aff", 128, NEXP, F32)
    t_top8 = sb("top8", 128, 8, F32)
    t_zap = sb("zap", 128, NEXP, F32)
    t_wmask = sb("wmask", 128, NEXP, F32)
    t_wsum = sb("wsum", 128, 1, F32)
    t_wrec = sb("wrec", 128, 1, F32)
    t_ws = sb("ws", 128, NTT * NEXP, F32)
    t_wT = sb("wTm", NEXP, S, BF16)
    t_urec = sb("urec", 128, 1, F32)
    t_outs = [sb(f"outs_{i}", 128, 1024, F32) for i in range(2)]
    t_wstage = sb("wstage", 128, 2 * NEXP, BF16)
    t_mscr = sb("mscr", 128, 1024, BF16)
    t_sqqs = [sb(f"sqq{i}", 128, 512, BF16) for i in range(2)]
    t_rr = sb("rr", 128, 512, BF16)
    t_ones64 = sb("ones64", 128, 64, BF16)

    def R_outs(blo, bhi):
        return ("outs", blo, bhi)

    def y1nbQ(tl):
        return t_outs[tl // 2].bitcast(BF16)[:, (tl % 2) * 1024:(tl % 2) * 1024 + 1024]

    def R_y1nbQ(tl, lo=0, hi=1024):
        base = (tl // 2) * 4096 + (tl % 2) * 2048
        return ("outs", base + 2 * lo, base + 2 * hi)

    def R(name):
        return (name, 0, 1 << 30)

    def Rws(tt, e0, e1):
        return ("ws", 4 * (tt * NEXP + e0), 4 * (tt * NEXP + e1))

    def RwT(t0, t1):
        return ("wT", 2 * t0, 2 * t1)

    psA = e_(nc.psum_tensor("psA", [128, 512], F32)).ap()
    psB = e_(nc.psum_tensor("psB", [128, 512], F32)).ap()
    psC = e_(nc.psum_tensor("psC", [128, 512], F32)).ap()
    psD = e_(nc.psum_tensor("psD", [128, 512], F32)).ap()
    psE = e_(nc.psum_tensor("psE", [128, 1024], F32)).ap()
    psF = e_(nc.psum_tensor("psF", [128, 1024], F32)).ap()
    ps1 = [psA, psB, psC, psD,
           psE[:, 0:512], psE[:, 512:1024], psF[:, 0:512], psF[:, 512:1024]]
    ps1b = [p.bitcast(BF16) for p in ps1]
    ps1r = [R("psA"), R("psB"), R("psC"), R("psD"),
            ("psE", 0, 1), ("psE", 1, 2), ("psF", 0, 1), ("psF", 1, 2)]
    R_psE = ("psE", 0, 2)
    R_psF = ("psF", 0, 2)

    P = Plan()

    def dma_g(dst, src, r=(), w=(), tag=""):
        P.op("gpsimd", lambda g: g.dma_start(dst, src), r=r, w=w, dma=True, tag=tag)

    def dma_s(dst, src, r=(), w=(), tag=""):
        P.op("sync", lambda s: s.dma_start(dst, src), r=r, w=w, dma=True, tag=tag)

    dma_g(t_ident[:, :], d_ident, [], [R("ident")], "ident")
    P.op("vector", lambda v: v.memset(t_ones_r[:, :], 1.0), w=[R("ones_r")])
    P.op("vector", lambda v: v.memset(t_oblk[:, :], 0.0), w=[R("oblk")])
    P.op("vector", lambda v: v.memset(t_oblk[0:64, 0:1], 1.0),
         r=[R("oblk")], w=[("oblk", 0, 0)])
    P.op("vector", lambda v: v.memset(t_oblk[64:128, 32:33], 1.0),
         r=[R("oblk")], w=[("oblk", 0, 0)])
    P.op("vector", lambda v: v.memset(t_eps[:, :], 1e-5), w=[R("eps")])
    P.op("vector", lambda v: v.memset(t_onecol[:, :], 1.0), w=[R("onecol")])
    P.op("vector", lambda v: v.memset(t_ones64[:, :], 1.0), w=[R("ones64")])
    P.op("vector", lambda v: v.memset(
        t_vbuf[:, 0:NH * 1040].rearrange("p (n c) -> p n c", c=65)[:, :, 64:65], 1.0),
        w=[("ar1", 0, 2 * NH * 1040)], tag="vones")

    dma_g(t_br[:, :], d_br, [], [R("br")], "br")
    dma_g(t_bds[:, :], d_bds, [], [R("bds")], "bds")
    dma_g(t_bd[:, :], d_bd, [], [R("bd")], "bd")
    dma_g(t_wr[:, :].rearrange("p (k e) -> p k e", e=NEXP),
          d_wr.rearrange("(k p) e -> p k e", p=128), [], [R("wr")], "wr")

    t_bstage = t_wT[:, 0:512]
    for nm, src, dstT in (("bg", d_bg, t_bgT), ("bu", d_bu, t_buT)):
        dma_g(t_bstage, src, [], [RwT(0, 512)], nm)
        for mt in range(4):
            pi = mt
            P.op("tensor", lambda t, mt=mt, pi=pi: t.transpose(
                ps1b[pi][:, 0:32], t_bstage[:, mt * 128:(mt + 1) * 128],
                t_ident[0:32, 0:32]),
                r=[RwT(0, 512), R("ident")], w=[ps1r[pi]])
            P.op("scalar", lambda s, mt=mt, pi=pi, dstT=dstT: s.copy(
                dstT.rearrange("p (e m) -> p e m", m=4)[:, :, mt:mt + 1],
                ps1b[pi][:, 0:32].rearrange("p (e m) -> p e m", m=1)),
                r=[ps1r[pi]], w=[(id(dstT).__str__(), 0, 1)], tag=nm + "T")

    def R_bT(dstT):
        return (id(dstT).__str__(), 0, 1)

    for nm, src, dstT in (("bgs", d_bgs, t_bgsT), ("bus", d_bus, t_busT)):
        dma_g(t_bstage[0:1, :], src, [], [RwT(0, 512)], nm)
        for mt in range(4):
            pi = mt
            P.op("tensor", lambda t, mt=mt, pi=pi: t.transpose(
                ps1b[pi][:, 0:1], t_bstage[0:1, mt * 128:(mt + 1) * 128],
                t_ident[0:1, 0:1]),
                r=[RwT(0, 512), R("ident")], w=[ps1r[pi]])
            P.op("scalar", lambda s, mt=mt, pi=pi, dstT=dstT: s.copy(
                dstT[:, mt:mt + 1], ps1b[pi][:, 0:1]),
                r=[ps1r[pi]], w=[R_bT(dstT)], tag=nm + "T")

    for wi, dsrc in ((0, d_wq), (1, d_wk)):
        for kt in range(NKT):
            st = t_xf32[kt % 2]
            skey = ("xf32", kt % 2, kt % 2 + 1)
            dma_g(st[:, :], dsrc[kt * 128:(kt + 1) * 128, :], [], [skey], "wqkld")
            P.op("vector", lambda v, st=st: v.tensor_reduce(
                t_means[:, :].rearrange("p (g o) -> p g o", o=1),
                st[:, :].rearrange("p (g c) -> p g c", c=64), AX.X, ALU.add),
                r=[skey], w=[R("means")])
            P.op("vector", lambda v: v.tensor_scalar(
                t_means[:, :], t_means[:, :], 1.0 / 64.0, None, ALU.mult),
                r=[R("means")], w=[R("means")])
            for g in range(16):
                P.op("vector", lambda v, st=st, g=g: v.tensor_scalar(
                    t_lnt[:, g * 64:(g + 1) * 64], st[:, g * 64:(g + 1) * 64],
                    t_means[:, g:g + 1], None, ALU.subtract),
                    r=[skey, R("means")], w=[("y1nb", g, g + 1)])
            dma_s(d_swqk[wi, kt * 128:(kt + 1) * 128, :], t_lnt[:, :],
                  [("y1nb", 0, 16)], [("swqk", wi * 8 + kt, wi * 8 + kt + 1)], "wqkst")

    def emit_A_dma(bn, tt, xr, rx, queue):
        (dma_s if queue == "sync" else dma_g)(
            xr[:, :], d_x[bn, tt * 128:(tt + 1) * 128, :], [], [rx], "xld")

    def emit_A_tr(tt, xr, rx):
        for et in range(NKT):
            pi = (tt * NKT + et) % 4
            P.op("tensor", lambda t, xr=xr, et=et, pi=pi: t.transpose(
                ps1b[pi][:, 0:128], xr[:, et * 128:(et + 1) * 128], t_ident),
                r=[rx, R("ident")], w=[ps1r[pi]], tag="xT")
            P.op("scalar", lambda s, et=et, tt=tt, pi=pi: s.copy(
                t_xT[:, et * 2048 + tt * 128: et * 2048 + (tt + 1) * 128],
                ps1b[pi][:, 0:128]),
                r=[ps1r[pi]], w=[R_xT(et, tt * 128, (tt + 1) * 128)])

    def emit_A_tile(bn, tt, xr, rx):
        emit_A_dma(bn, tt, xr, rx, "gpsimd")
        emit_A_tr(tt, xr, rx)

    pending_finish = []
    for b in range(n_batch):
        if b == 0:
            for tt in range(NTT):
                emit_A_tile(0, tt, t_xroll[tt % 2], R_xroll(tt % 2))

        for par in range(2):
            for i in range(4):
                P.op("vector", lambda v, par=par, i=i: v.memset(
                    t_qhs(par, i)[64:128, :], 0.0),
                    w=[R_qhs(par, i)], tag="qhz")

        def c1_maker(p):
            par = p % 2
            pending = []
            pend_ev = []
            state = {"j": 0}

            def flush():
                while pend_ev:
                    pend_ev.pop(0)()
                while pending:
                    pending.pop(0)()

            def mid():
                if pend_ev:
                    pend_ev.pop(0)()

            def emit_next():
                if state["j"] < 8:
                    emit_slice(state["j"])
                    state["j"] += 1

            def emit_slice(j):
                flush()
                sc, wi = j >> 1, j & 1
                wt, rwt = t_wqk(par, wi), R_wqk(par, wi)
                if j <= 1:
                    dma_s(wt[:, :].rearrange("p (k m) -> p k m", m=128),
                          d_swqk[wi].rearrange("(k p) m -> p k m", p=128)
                          [:, :, p * 128:(p + 1) * 128],
                          [("swqk", wi * 8, wi * 8 + 8)], [rwt], "wqkpld")
                ps, rps = (psC, R("psC")) if wi == 0 else (psD, R("psD"))
                for kt in range(NKT):
                    P.op("tensor", lambda t, ps=ps, wt=wt, kt=kt, sc=sc: t.matmul(
                        ps[:, :], wt[:, kt * 128:(kt + 1) * 128],
                        t_xT[:, kt * 2048 + sc * 512: kt * 2048 + (sc + 1) * 512],
                        start=(kt == 0), stop=(kt == NKT - 1)),
                        r=[R_xT(kt, sc * 512, (sc + 1) * 512), rwt], w=[rps],
                        tag="qkmm")
                if wi == 0:
                    sqa, rsa = t_sqqs[0][0:64, :], R("sqq0")
                    sqb, rsb = t_sqqs[1][0:64, :], R("sqq1")
                else:
                    sqa, rsa = t_mscr[0:64, 0:512], ("mscr", 0, 1024)
                    sqb, rsb = t_mscr[0:64, 512:1024], ("mscr", 1024, 2048)
                qa, ra = t_qhs(par, 2 * wi + 0), R_qhs(par, 2 * wi + 0)
                qb, rb = t_qhs(par, 2 * wi + 1), R_qhs(par, 2 * wi + 1)

                def evac(ps=ps, rps=rps, qa=qa, ra=ra, qb=qb, rb=rb, sc=sc,
                         sqa=sqa, rsa=rsa, sqb=sqb, rsb=rsb):
                    P.op("scalar", lambda s: s.copy(
                        qa[0:64, sc * 512:(sc + 1) * 512], ps[0:64, :]),
                        r=[rps], w=[ra], tag="qev")
                    P.op("scalar", lambda s: s.copy(
                        qb[0:64, sc * 512:(sc + 1) * 512], ps[64:128, :]),
                        r=[rps], w=[rb], tag="qev")
                    P.op("gpsimd", lambda g: g.tensor_mul(
                        sqa[:, :], qa[0:64, sc * 512:(sc + 1) * 512],
                        qa[0:64, sc * 512:(sc + 1) * 512]),
                        r=[ra], w=[rsa], tag="gsq")
                    P.op("gpsimd", lambda g: g.tensor_mul(
                        sqb[:, :], qb[0:64, sc * 512:(sc + 1) * 512],
                        qb[0:64, sc * 512:(sc + 1) * 512]),
                        r=[rb], w=[rsb], tag="gsq")
                pend_ev.append(evac)

                def var(ps=ps, rps=rps, qa=qa, ra=ra, qb=qb, rb=rb, sc=sc,
                        sqa=sqa, rsa=rsa, sqb=sqb, rsb=rsb):
                    P.op("tensor", lambda t: t.matmul(
                        ps[0:1, :], t_ones64[0:64, 0:1], sqa[:, :],
                        start=True, stop=False),
                        r=[rsa, R("ones64")], w=[rps], tag="varmm")
                    P.op("tensor", lambda t: t.matmul(
                        ps[32:33, :], t_ones64[0:64, 0:1], sqb[:, :],
                        start=False, stop=True),
                        r=[rsb, R("ones64")], w=[rps], tag="varmm")
                    P.op("scalar", lambda s: s.activation(
                        qa[64:65, sc * 512:(sc + 1) * 512], ps[0:1, :], AF.Copy,
                        scale=1.0 / 63.0),
                        r=[rps], w=[ra], tag="varev")
                    P.op("scalar", lambda s: s.activation(
                        qb[64:65, sc * 512:(sc + 1) * 512], ps[32:33, :], AF.Copy,
                        scale=1.0 / 63.0),
                        r=[rps], w=[rb], tag="varev")
                pending.append(var)
            return emit_next, flush, mid

        c1 = [c1_maker(p) for p in range(NPAIR)]

        dma_g(t_wv1[:, :].rearrange("p (k i) -> p k i", i=512),
              d_wv.rearrange("(k p) m -> p k m", p=128)[:, :, 0:512],
              [], [R_wv1], "wvld")
        for j in range(8):
            c1[0][0]()
            if pending_finish:
                pending_finish.pop(0)()
            c1[0][2]()
        assert not pending_finish
        P.op("vector", lambda v: v.memset(
            t_vbuf[:, 0:NH * 1040].rearrange("p (n c) -> p n c", c=65)[:, :, 64:65], 1.0),
            w=[("ar1", 0, 2 * NH * 1040)], tag="vones")
        for ec in range(2):
            if ec == 1:
                dma_g(t_wv1[:, :].rearrange("p (k i) -> p k i", i=512),
                      d_wv.rearrange("(k p) m -> p k m", p=128)[:, :, 512:1024],
                      [], [R_wv1], "wvld")
                for _ in range(3):
                    c1[1][0]()
            for tt in range(NTT):
                ps, rps = ps1[tt % 2], ps1r[tt % 2]
                for kt in range(NKT):
                    P.op("tensor", lambda t, ps=ps, kt=kt, tt=tt: t.matmul(
                        ps[:, :], t_xT[:, kt * 2048 + tt * 128: kt * 2048 + (tt + 1) * 128],
                        t_wv1[:, kt * 512:(kt + 1) * 512],
                        start=(kt == 0), stop=(kt == NKT - 1)),
                        r=[R_xT(kt, tt * 128, (tt + 1) * 128), R_wv1], w=[rps], tag="vmm")
                P.op("vector", lambda v, ps=ps, ec=ec, tt=tt: v.tensor_copy(
                    t_vbuf.rearrange("p (j c) -> p j c", c=1040)
                    [:, ec * 8:(ec + 1) * 8, tt * 65: tt * 65 + 64],
                    ps[:, :].rearrange("p (j c) -> p j c", c=64)),
                    r=[rps], w=[R_vbuf(ec * 8 + j, tt) for j in range(8)],
                    tag="vev")

        def emit_umm(h, tp):
            at = t_ATr[tp % 2]
            for half in range(2):
                tt = 2 * tp + half
                P.op("tensor", lambda t, h=h, tt=tt, at=at, half=half: t.matmul(
                    psA[0:65, 0:512],
                    t_vbuf[:, h * 1040 + tt * 65: h * 1040 + (tt + 1) * 65],
                    at[:, half * 512:(half + 1) * 512],
                    start=(tt == 0), stop=(tt == NTT - 1)),
                    r=[R_vbuf(h, tt), R_ATr(tp % 2)], w=[ps1r[0]], tag="ummv")

        def emit_c2_step(hp, hl, sc, mid_cb=None):
            par = hp % 2
            h = hp * 2 + hl
            qh, rqh = t_qhs(par, hl), R_qhs(par, hl)
            kh, rkh = t_qhs(par, 2 + hl), R_qhs(par, 2 + hl)
            for tp in range(8):
                if tp == 5 and mid_cb:
                    mid_cb()
                wps = psE if tp % 2 == 0 else psF
                base = 4 if tp % 2 == 0 else 6
                for half in range(2):
                    tt = 2 * tp + half
                    P.op("tensor", lambda t, wps=wps, half=half, kh=kh, qh=qh,
                         tt=tt, sc=sc: t.matmul(
                             wps[:, half * 512:(half + 1) * 512],
                             kh[:, tt * 128:(tt + 1) * 128],
                             qh[:, sc * 512:(sc + 1) * 512], start=True, stop=True),
                        r=[rkh, rqh], w=[ps1r[base + half]], tag="smm")
                if tp >= 2:
                    emit_umm(h, tp - 2)
                at = t_ATr[tp % 2]
                P.op("scalar", lambda s, at=at, wps=wps: s.activation(
                    at[:, :], wps[:, 0:1024], AF.Exp, scale=1.0 / 64.0),
                    r=[ps1r[base], ps1r[base + 1]], w=[R_ATr(tp % 2)], tag="exp")
            emit_umm(h, 6)
            emit_umm(h, 7)
            if hl == 0:
                P.op("scalar", lambda s: s.copy(
                    t_ustg[0:64, :], psA[0:64, 0:512]),
                    r=[ps1r[0]], w=[R_ustg], tag="ucp")
            else:
                P.op("scalar", lambda s: s.copy(
                    t_ustg[64:128, :], psA[0:64, 0:512]),
                    r=[ps1r[0]], w=[R_ustg], tag="ucp")
            P.op("scalar", lambda s: s.activation(
                t_xf32[0][64:65, 0:512], psA[64:65, 0:512], AF.Ln),
                r=[ps1r[0]], w=[("xf32", 0, 1)], tag="rln")
            P.op("scalar", lambda s: s.activation(
                t_rr[64:65, 0:512], t_xf32[0][64:65, 0:512], AF.Exp, scale=-1.0),
                r=[("xf32", 0, 1)], w=[R("rr")], tag="rrec")

            def tail(hl=hl, hp=hp, sc=sc):
                P.op("tensor", lambda t: t.matmul(
                    psB[hl * 64:(hl + 1) * 64, 0:512], t_ones64[64:65, 0:64],
                    t_rr[64:65, 0:512], start=True, stop=True),
                    r=[R("ones64"), R("rr")], w=[R("psB")], tag="bcmm")
                P.op("scalar", lambda s: s.copy(
                    t_rbc[hl * 64:(hl + 1) * 64, :], psB[hl * 64:(hl + 1) * 64, :]),
                    r=[R("psB")], w=[R_rbc], tag="rbccp")
                P.op("vector", lambda v: v.tensor_tensor(
                    t_aoT[hl * 64:(hl + 1) * 64, hp * 2048 + sc * 512: hp * 2048 + (sc + 1) * 512],
                    t_ustg[hl * 64:(hl + 1) * 64, :], t_rbc[hl * 64:(hl + 1) * 64, :],
                    ALU.mult),
                    r=[R_ustg, R_rbc], w=[R_aoT(hp, sc * 512, (sc + 1) * 512)],
                    tag="aomul")
            return tail

        for hp in range(NPAIR):
            c1[hp][1]()
            for hl in range(2):
                for sc in range(4):
                    tail = emit_c2_step(
                        hp, hl, sc, c1[hp + 1][2] if hp + 1 < NPAIR else None)
                    if hp + 1 < NPAIR:
                        c1[hp + 1][0]()
                    tail()
            if hp == 6:
                for jw in range(2):
                    dma_g(t_wo2[jw][:, :].rearrange("p (k i) -> p k i", i=512),
                          d_wo.rearrange("(k p) m -> p k m", p=128)
                          [:, :, jw * 512:(jw + 1) * 512],
                          [], [R_wo2(jw)], "wold")


        def emit_router(tc):
            pi = (2 * (tc % 4)) % 4
            pr, rpr = ps1[pi], ps1r[pi]
            for kt in range(NKT):
                P.op("tensor", lambda t, kt=kt, tc=tc, pr=pr: t.matmul(
                    pr[:, 0:NEXP],
                    t_y1nT[:, kt * 2048 + tc * 128: kt * 2048 + (tc + 1) * 128],
                    t_wr[:, kt * NEXP:(kt + 1) * NEXP],
                    start=(kt == 0), stop=False),
                    r=[R_y1nT(kt, tc * 128, (tc + 1) * 128), R("wr")], w=[rpr],
                    tag="rmm")
            P.op("tensor", lambda t, pr=pr: t.matmul(
                pr[:, 0:NEXP], t_ones_r[:, :], t_br[:, :], start=False, stop=True),
                r=[R("ones_r"), R("br")], w=[rpr], tag="rbias")
            P.op("scalar", lambda s, pr=pr: s.activation(
                t_aff[:, :], pr[:, 0:NEXP], AF.Sigmoid),
                r=[rpr], w=[R("aff")], tag="sig")
            P.op("vector", lambda v: v.max(out=t_top8[:, :], in_=t_aff[:, :]),
                 r=[R("aff")], w=[R("top8")])
            P.op("vector", lambda v: v.match_replace(
                out=t_zap[:, :], in_to_replace=t_top8[:, :], in_values=t_aff[:, :],
                imm_value=0.0),
                r=[R("aff"), R("top8")], w=[R("zap"), R("top8")])
            P.op("vector", lambda v: v.tensor_sub(
                t_wmask[:, :], t_aff[:, :], t_zap[:, :]),
                r=[R("aff"), R("zap")], w=[R("wmask")])
            P.op("vector", lambda v: v.tensor_reduce(
                t_wsum[:, :], t_top8[:, :], AX.X, ALU.add),
                r=[R("top8")], w=[R("wsum")])
            P.op("vector", lambda v: v.tensor_scalar(
                t_wsum[:, :], t_wsum[:, :], 1e-9, None, ALU.add),
                r=[R("wsum")], w=[R("wsum")])
            P.op("vector", lambda v: v.reciprocal(t_wrec[:, :], t_wsum[:, :]),
                 r=[R("wsum")], w=[R("wrec")])
            P.op("vector", lambda v, tc=tc: v.tensor_scalar(
                t_ws[:, tc * NEXP:(tc + 1) * NEXP], t_wmask[:, :], t_wrec[:, :],
                None, ALU.mult),
                r=[R("wmask"), R("wrec")], w=[Rws(tc, 0, NEXP)], tag="wnorm")
            sl = tc % 2
            P.op("vector", lambda v, tc=tc, sl=sl: v.tensor_copy(
                t_wstage[:, sl * NEXP:(sl + 1) * NEXP],
                t_ws[:, tc * NEXP:(tc + 1) * NEXP]),
                r=[Rws(tc, 0, NEXP)], w=[("wstage", sl, sl + 1)], tag="wbf")

        def emit_wT(tc):
            sl = tc % 2
            pj = (2 * (tc % 4) + 1) % 4
            P.op("tensor", lambda t, sl=sl, pj=pj: t.transpose(
                ps1b[pj][0:NEXP, 0:128], t_wstage[:, sl * NEXP:(sl + 1) * NEXP],
                t_ident),
                r=[("wstage", sl, sl + 1), R("ident")], w=[ps1r[pj]], tag="wTmm")
            P.op("scalar", lambda s, tc=tc, pj=pj: s.copy(
                t_wT[:, tc * 128:(tc + 1) * 128], ps1b[pj][0:NEXP, 0:128]),
                r=[ps1r[pj]], w=[RwT(tc * 128, (tc + 1) * 128)], tag="wTev")

        def emit_deferred(tc):
            if tc >= 1:
                emit_wT(tc - 1)
            tl = tc % 4
            for et in range(NKT):
                pj = 4 + et % 4
                P.op("tensor", lambda t, et=et, tl=tl, pj=pj: t.transpose(
                    ps1b[pj][:, 0:128], y1nbQ(tl)[:, et * 128:(et + 1) * 128],
                    t_ident),
                    r=[R_y1nbQ(tl, et * 128, (et + 1) * 128), R("ident")],
                    w=[ps1r[pj]], tag="y1T")
                P.op("scalar", lambda s, et=et, tc=tc, pj=pj: s.copy(
                    t_y1nT[:, et * 2048 + tc * 128: et * 2048 + (tc + 1) * 128],
                    ps1b[pj][:, 0:128]),
                    r=[ps1r[pj]], w=[R_y1nT(et, tc * 128, (tc + 1) * 128)],
                    tag="y1Tev")
            emit_router(tc)
            if tc == NTT - 1:
                emit_wT(tc)

        dma_s(t_xf32[0][:, :], d_x[b, 0:128, :], [], [("xf32", 0, 1)], "xres")
        d4flush = [(lambda tl=tl: emit_deferred(12 + tl)) for tl in range(4)]
        for sc in range(4):
            for tl in range(4):
                if sc > 0:
                    emit_deferred((sc - 1) * 4 + tl)
                if sc < 4:
                    tt = sc * 4 + tl
                    if tt + 1 < NTT:
                        p2n = (tt + 1) % 2
                        dma_s(t_xf32[p2n][:, :], d_x[b, (tt + 1) * 128:(tt + 2) * 128, :],
                              [], [("xf32", p2n, p2n + 1)], "xres")
                    ysel, Rysel = ((t_y1sc, R_y1sc) if sc % 2 == 1
                                   else (t_y1scB, R_y1scB))
                    y1 = ysel[tl]
                    ry1 = Rysel(tl)
                    rx = ("xf32", tt % 2, tt % 2 + 1)
                    for half in range(2):
                        pi = (2 * tl + half) % 4
                        ps, rps = ps1[pi], ps1r[pi]
                        wvb, rwv = t_wo2[half], R_wo2(half)
                        for kt in range(NKT):
                            P.op("tensor", lambda t, ps=ps, kt=kt, tt=tt, wvb=wvb: t.matmul(
                                ps[:, :],
                                t_aoT[:, kt * 2048 + tt * 128: kt * 2048 + (tt + 1) * 128],
                                wvb[:, kt * 512:(kt + 1) * 512],
                                start=(kt == 0), stop=(kt == NKT - 1)),
                                r=[R_aoT(kt, tt * 128, (tt + 1) * 128), rwv],
                                w=[rps], tag="pmm")
                        P.op("vector", lambda v, ps=ps, y1=y1, half=half,
                             xr=t_xf32[tt % 2]: v.tensor_add(
                                 y1[:, half * 512:(half + 1) * 512], ps[:, :],
                                 xr[:, half * 512:(half + 1) * 512]),
                             r=[rps, rx], w=[Rysel(tl, half * 512, (half + 1) * 512)],
                             tag="resadd")
                    p2 = tl % 2
                    negm, vsum, rstd = t_negm2[p2], t_vsum2[p2], t_rstd2[p2]
                    rng, rvs, rrs = (f"negm{p2}", 0, 1), (f"vsum{p2}", 0, 1), (f"rstd{p2}", 0, 1)
                    P.op("vector", lambda v, y1=y1, negm=negm: v.tensor_scalar(
                        t_mscr[:, :], y1[:, :], -1.0 / 1024.0, None,
                        ALU.mult, ALU.add, accum_out=negm[:, :]),
                        r=[ry1], w=[R("mscr"), rng], tag="lnmean")
                    P.op("scalar", lambda s, y1=y1, negm=negm, vsum=vsum: s.activation(
                        t_y1nb[:, :], y1[:, :], AF.Square, bias=negm[:, :],
                        accum_out=vsum[:, :]),
                        r=[ry1, rng], w=[R("y1nb"), rvs], tag="lnsq")
                    P.op("scalar", lambda s, vsum=vsum, rstd=rstd: s.activation(
                        rstd[:, :], vsum[:, :], AF.Sqrt, bias=t_eps[:, :],
                        scale=1.0 / 1024.0),
                        r=[rvs, R("eps")], w=[rrs], tag="lnsqrt")
                    P.op("vector", lambda v, rstd=rstd: v.reciprocal(rstd[:, :], rstd[:, :]),
                         r=[rrs], w=[rrs])
                    P.op("vector", lambda v, y1=y1, negm=negm, rstd=rstd, tl=tl: v.tensor_scalar(
                        y1nbQ(tl)[:, :], y1[:, :], negm[:, :], rstd[:, :],
                        ALU.add, ALU.mult),
                        r=[ry1, rng, rrs], w=[R_y1nbQ(tl)], tag="ln1n")

        def a_hook(k):
            if k >= 1:
                emit_A_tr(k - 1, t_xroll1, R_xroll1)
            if k < NTT:
                emit_A_dma(b + 1, k, t_xroll1, R_xroll1, "gpsimd")
        a_next = ([(lambda k=k: a_hook(k)) for k in range(NTT + 1)]
                  if b + 1 < n_batch else [])
        for g in range(NG):
            first = (b == 0 and g == 0)

            def wsrc(ve):
                if ve == 0:
                    return (d_wgs, d_wus, d_wds, d_swgs, d_swus, d_swds, NEXP * 3)
                e = ve - 1
                return (d_wg[e], d_wu[e], d_wd[e], d_swg[e], d_swu[e], d_swd[e],
                        e * 3)

            def plan_expert_dma(ve):
                sl = ve % 2
                sg_, su_, sd_, cg_, cu_, cd_, sx = wsrc(ve)
                if first:
                    dma_g(t_wge[sl][:, :].rearrange("p (k i) -> p k i", i=512),
                          sg_.rearrange("(k p) i -> p k i", p=128),
                          [], [R_wge(sl)], "wgld")
                    dma_s(cg_.rearrange("(k p) i -> p k i", p=128),
                          t_wge[sl][:, :].rearrange("p (k i) -> p k i", i=512),
                          [R_wge(sl)], [("sexp", sx, sx + 1)], "wgst")
                    dma_g(t_wue[sl][:, :].rearrange("p (k i) -> p k i", i=512),
                          su_.rearrange("(k p) i -> p k i", p=128),
                          [], [R_wue(sl)], "wuld")
                    dma_s(cu_.rearrange("(k p) i -> p k i", p=128),
                          t_wue[sl][:, :].rearrange("p (k i) -> p k i", i=512),
                          [R_wue(sl)], [("sexp", sx + 1, sx + 2)], "wust")
                    dma_g(t_wde[sl][:, :].rearrange("p (k m) -> p k m", m=1024),
                          sd_.rearrange("(k p) m -> p k m", p=128),
                          [], [R_wde(sl)], "wdld")
                    dma_s(cd_.rearrange("(k p) m -> p k m", p=128),
                          t_wde[sl][:, :].rearrange("p (k m) -> p k m", m=1024),
                          [R_wde(sl)], [("sexp", sx + 2, sx + 3)], "wdst")
                else:
                    dma_g(t_wge[sl][:, :].rearrange("p (k i) -> p k i", i=512),
                          cg_.rearrange("(k p) i -> p k i", p=128),
                          [("sexp", sx, sx + 1)], [R_wge(sl)], "wgld")
                    dma_g(t_wue[sl][:, :].rearrange("p (k i) -> p k i", i=512),
                          cu_.rearrange("(k p) i -> p k i", p=128),
                          [("sexp", sx + 1, sx + 2)], [R_wue(sl)], "wuld")
                    dma_g(t_wde[sl][:, :].rearrange("p (k m) -> p k m", m=1024),
                          cd_.rearrange("(k p) m -> p k m", p=128),
                          [("sexp", sx + 2, sx + 3)], [R_wde(sl)], "wdld")

            def emit_gu(ve, filler=None):
                sl = ve % 2
                wgt, rwg = t_wge[sl], R_wge(sl)
                wut, rwu = t_wue[sl], R_wue(sl)
                if ve == 0:
                    bgc, buc = t_bgsT, t_busT
                    bge = bue = 0
                else:
                    bgc, buc = t_bgT, t_buT
                    bge = bue = (ve - 1) * 4
                act = t_act[ve % 2]
                ract = R_act(ve % 2)
                for sub in range(2):
                    t0 = g * GROUP + sub * 512
                    for mt in range(4):
                        ps, rps = ps1[mt % 2], ps1r[mt % 2]
                        ps2, rps2 = ps1[2 + mt % 2], ps1r[2 + mt % 2]
                        for kt in range(NKT):
                            P.op("tensor", lambda t, ps=ps, wgt=wgt, kt=kt, mt=mt, t0=t0: t.matmul(
                                ps[:, :], wgt[:, kt * 512 + mt * 128: kt * 512 + (mt + 1) * 128],
                                t_y1nT[:, kt * 2048 + t0: kt * 2048 + t0 + 512],
                                start=(kt == 0), stop=(kt == NKT - 1)),
                                r=[R_y1nT(kt, t0, t0 + 512), rwg], w=[rps], tag="gmm")
                        for kt in range(NKT):
                            P.op("tensor", lambda t, ps2=ps2, wut=wut, kt=kt, mt=mt, t0=t0: t.matmul(
                                ps2[:, :], wut[:, kt * 512 + mt * 128: kt * 512 + (mt + 1) * 128],
                                t_y1nT[:, kt * 2048 + t0: kt * 2048 + t0 + 512],
                                start=(kt == 0), stop=(kt == NKT - 1)),
                                r=[R_y1nT(kt, t0, t0 + 512), rwu], w=[rps2], tag="umm2")
                        P.op("scalar", lambda s, ps=ps, mt=mt, bgc=bgc, bge=bge: s.activation(
                            t_silu[:, mt * 512:(mt + 1) * 512], ps[:, :], AF.Silu,
                            bias=bgc[:, bge + mt: bge + mt + 1]),
                            r=[rps, R_bT(bgc)], w=[("ar2", 49152 + mt * 1024, 49152 + (mt + 1) * 1024)],
                            tag="silu")
                        P.op("vector", lambda v, ps2=ps2, act=act, sub=sub, mt=mt,
                             buc=buc, bue=bue: v.scalar_tensor_tensor(
                                 act[:, sub * 2048 + mt * 512: sub * 2048 + (mt + 1) * 512],
                                 ps2[:, :], buc[:, bue + mt: bue + mt + 1],
                                 t_silu[:, mt * 512:(mt + 1) * 512],
                                 ALU.add, ALU.mult),
                             r=[rps2, R_bT(buc),
                                ("ar2", 49152 + mt * 1024, 49152 + (mt + 1) * 1024)],
                             w=[("ar3", (ve % 2) * 8192 + sub * 4096 + mt * 1024,
                                (ve % 2) * 8192 + sub * 4096 + (mt + 1) * 1024)],
                             tag="actm")
                        if filler:
                            filler.pop(0)()

            def emit_d(vd):
                wdt, rwd = t_wde[vd % 2], R_wde(vd % 2)
                act = t_act[vd % 2]
                ract = R_act(vd % 2)
                for tsub in range(GT):
                    sub, tl0 = tsub // 4, (tsub % 4) * 128
                    psd = psE if tsub % 2 == 0 else psF
                    rpsd = R_psE if tsub % 2 == 0 else R_psF
                    for it in range(4):
                        for hc in range(2):
                            P.op("tensor", lambda t, psd=psd, act=act, wdt=wdt,
                                 it=it, hc=hc, sub=sub, tl0=tl0: t.matmul(
                                     psd[:, hc * 512:(hc + 1) * 512],
                                     act[:, sub * 2048 + it * 512 + tl0: sub * 2048 + it * 512 + tl0 + 128],
                                     wdt[:, it * 1024 + hc * 512: it * 1024 + (hc + 1) * 512],
                                     start=(it == 0), stop=(it == 3)),
                                r=[ract, rwd], w=[rpsd], tag="dmm")
                    tt_g = g * GT + tsub
                    if vd == 0:
                        P.op("scalar", lambda s, psd=psd, tsub=tsub: s.copy(
                            t_racc[:, tsub * 1024:(tsub + 1) * 1024], psd[:, :]),
                            r=[rpsd], w=[R_racc(tsub)], tag="dcp")
                    else:
                        e = vd - 1
                        P.op("vector", lambda v, psd=psd, tsub=tsub, tt_g=tt_g, e=e:
                             v.scalar_tensor_tensor(
                                 t_racc[:, tsub * 1024:(tsub + 1) * 1024], psd[:, :],
                                 t_ws[:, tt_g * NEXP + e: tt_g * NEXP + e + 1],
                                 t_racc[:, tsub * 1024:(tsub + 1) * 1024],
                                 ALU.mult, ALU.add),
                             r=[rpsd, Rws(tt_g, e, e + 1), R_racc(tsub)],
                             w=[R_racc(tsub)], tag="dstt")

            def mk_finish(g, tsub, b=b):
                def go():
                    tt_g = g * GT + tsub
                    psd = psE if tsub % 2 == 0 else psF
                    rpsd = R_psE if tsub % 2 == 0 else R_psF
                    for hc in range(2):
                        P.op("tensor", lambda t, psd=psd, hc=hc, tt_g=tt_g: t.matmul(
                            psd[:, hc * 512:(hc + 1) * 512],
                            t_wT[:, tt_g * 128:(tt_g + 1) * 128],
                            t_bd[:, hc * 512:(hc + 1) * 512], start=True, stop=False),
                            r=[RwT(tt_g * 128, (tt_g + 1) * 128), R("bd")], w=[rpsd],
                            tag="bmm")
                        P.op("tensor", lambda t, psd=psd, hc=hc: t.matmul(
                            psd[:, hc * 512:(hc + 1) * 512], t_ones_r[:, :],
                            t_bds[:, hc * 512:(hc + 1) * 512], start=False, stop=False),
                            r=[R("ones_r"), R("bds")], w=[rpsd], tag="bmm2")
                        for et in range(4):
                            ec4 = hc * 4 + et
                            P.op("tensor", lambda t, psd=psd, hc=hc, et=et, ec4=ec4,
                                 tt_g=tt_g: t.matmul(
                                     psd[:, hc * 512 + et * 128: hc * 512 + (et + 1) * 128],
                                     t_y1nT[:, ec4 * 2048 + tt_g * 128: ec4 * 2048 + (tt_g + 1) * 128],
                                     t_ident, start=False, stop=(et == 3)),
                                r=[R_y1nT(ec4, tt_g * 128, (tt_g + 1) * 128), R("ident")],
                                w=[rpsd], tag="ymm")
                    P.op("vector", lambda v, psd=psd, tsub=tsub: v.tensor_add(
                        t_racc[:, tsub * 1024:(tsub + 1) * 1024], psd[:, :],
                        t_racc[:, tsub * 1024:(tsub + 1) * 1024]),
                        r=[rpsd, R_racc(tsub)], w=[R_racc(tsub)], tag="badd")
                    p2 = tsub % 2
                    oy = t_outs[p2]
                    roy = R_outs(p2 * 4096, (p2 + 1) * 4096)
                    negm, vsum, rstd = t_negm2[p2], t_vsum2[p2], t_rstd2[p2]
                    rng, rvs, rrs = (f"negm{p2}", 0, 1), (f"vsum{p2}", 0, 1), (f"rstd{p2}", 0, 1)
                    P.op("vector", lambda v, tsub=tsub, negm=negm: v.tensor_scalar(
                        t_mscr[:, :], t_racc[:, tsub * 1024:(tsub + 1) * 1024],
                        -1.0 / 1024.0, None, ALU.mult, ALU.add, accum_out=negm[:, :]),
                        r=[R_racc(tsub)], w=[R("mscr"), rng], tag="ln2mean")
                    P.op("scalar", lambda s, tsub=tsub, negm=negm, vsum=vsum: s.activation(
                        t_y1nb[:, :], t_racc[:, tsub * 1024:(tsub + 1) * 1024],
                        AF.Square, bias=negm[:, :], accum_out=vsum[:, :]),
                        r=[R_racc(tsub), rng], w=[R("y1nb"), rvs], tag="ln2sq")
                    P.op("scalar", lambda s, vsum=vsum, rstd=rstd: s.activation(
                        rstd[:, :], vsum[:, :], AF.Sqrt, bias=t_eps[:, :],
                        scale=1.0 / 1024.0),
                        r=[rvs, R("eps")], w=[rrs], tag="ln2sqrt")
                    P.op("vector", lambda v, rstd=rstd: v.reciprocal(rstd[:, :], rstd[:, :]),
                         r=[rrs], w=[rrs])
                    P.op("vector", lambda v, oy=oy, tsub=tsub, negm=negm, rstd=rstd: v.tensor_scalar(
                        oy[:, :], t_racc[:, tsub * 1024:(tsub + 1) * 1024],
                        negm[:, :], rstd[:, :], ALU.add, ALU.mult),
                        r=[R_racc(tsub), rng, rrs], w=[roy], tag="ln2n")
                    dma_s(d_out[b, tt_g * 128:(tt_g + 1) * 128, :], oy[:, :],
                          [roy], [], "outst")
                return go

            plan_expert_dma(0)
            plan_expert_dma(1)
            filler = d4flush if g == 0 else pending_finish
            for ve in range(NEXP + 2):
                if ve <= NEXP:
                    emit_gu(ve, filler if ve == 0 else None)
                if ve >= 1:
                    emit_d(ve - 1)
                if 1 <= ve < NEXP:
                    plan_expert_dma(ve + 1)
                if g == 1 and a_next and ve % 2 == 0:
                    a_next.pop(0)()
            pending_finish = [mk_finish(g, tsub) for tsub in range(GT)]

        if b == n_batch - 1:
            for go in pending_finish:
                go()
            pending_finish = []

        if b == 0:
            if "xT" in dbg_out:
                dma_s(dbg_out["xT"], t_xT[:, :], [R_xT(kt, 0, S) for kt in range(NKT)], [])
            if "aoT" in dbg_out:
                dma_s(dbg_out["aoT"], t_aoT[:, :], [R_aoT(hp, 0, S) for hp in range(NPAIR)], [])
            if "y1nT" in dbg_out:
                dma_s(dbg_out["y1nT"], t_y1nT[:, 0:NKT * S],
                      [R_y1nT(kt, 0, S) for kt in range(NKT)], [])
            if "ws" in dbg_out:
                dma_s(dbg_out["ws"], t_ws[:, :], [Rws(tt, 0, NEXP) for tt in range(NTT)], [])

    P.finalize()
    sems = {}
    block = None
    sem_objs = []
    for nm in ("tensor", "scalar", "vector", "gpsimd", "sync", "dma_gpsimd", "dma_sync"):
        h = ctx.enter_context(nc.semaphore("sem_" + nm))
        sems[nm] = h
    block = ctx.enter_context(nc.Block())

    @block.sync
    def _(sy):
        P.emit_stream("sync", sy, sems)
        if P.dma_totals["sync"]:
            sy.wait_ge(sems["dma_sync"], P.dma_totals["sync"])

    @block.gpsimd
    def _(gp):
        P.emit_stream("gpsimd", gp, sems)
        if P.dma_totals["gpsimd"]:
            gp.wait_ge(sems["dma_gpsimd"], P.dma_totals["gpsimd"])

    @block.tensor
    def _(t):
        P.emit_stream("tensor", t, sems)

    @block.scalar
    def _(s):
        P.emit_stream("scalar", s, sems)

    @block.vector
    def _(v):
        P.emit_stream("vector", v, sems)

    ctx.close()
    return nc


def kernel(L0_attn_x, L0_attn_Wq, L0_attn_Wk, L0_attn_Wv, L0_attn_Wo,
           L0_ffn_Wr, L0_ffn_br, L0_ffn_Wg_s, L0_ffn_bg_s, L0_ffn_Wu_s,
           L0_ffn_bu_s, L0_ffn_Wd_s, L0_ffn_bd_s, L0_ffn_Wg, L0_ffn_bg,
           L0_ffn_Wu, L0_ffn_bu, L0_ffn_Wd, L0_ffn_bd):
    x = np.ascontiguousarray(np.asarray(L0_attn_x, dtype=np.float32))
    ident = np.eye(128, dtype=np.float32)

    def f(a, shape=None):
        a = np.ascontiguousarray(np.asarray(a, dtype=np.float32))
        return a.reshape(shape) if shape is not None else a

    common = {
        "wq": f(L0_attn_Wq), "wk": f(L0_attn_Wk), "wv": f(L0_attn_Wv),
        "wo": f(L0_attn_Wo), "wr": f(L0_ffn_Wr), "br": f(L0_ffn_br, (1, NEXP)),
        "wgs": f(L0_ffn_Wg_s), "bgs": f(L0_ffn_bg_s, (1, NI)),
        "wus": f(L0_ffn_Wu_s), "bus": f(L0_ffn_bu_s, (1, NI)),
        "wds": f(L0_ffn_Wd_s), "bds": f(L0_ffn_bd_s, (1, E)),
        "wg": f(L0_ffn_Wg), "bg": f(L0_ffn_bg), "wu": f(L0_ffn_Wu),
        "bu": f(L0_ffn_bu), "wd": f(L0_ffn_Wd), "bd": f(L0_ffn_bd),
        "ident": ident,
    }
    nc = build_program(NB)
    in_maps = []
    for c in range(8):
        m = dict(common)
        m["x_in"] = x[c * NB:(c + 1) * NB]
        in_maps.append(m)
    res = run_bass_kernel_spmd(nc, in_maps, core_ids=list(range(8)))
    return np.concatenate([r["y_out"] for r in res.results], axis=0)
